# version 23
# baseline (speedup 1.0000x reference)
import numpy as np
import ml_dtypes
import concourse.bass as bass
import concourse.mybir as mybir
from concourse.bass_utils import run_bass_kernel_spmd
from contextlib import ExitStack

F32 = mybir.dt.float32
F32R = mybir.dt.float32r
BF16 = mybir.dt.bfloat16
AF = mybir.ActivationFunctionType
ALU = mybir.AluOpType
AX = mybir.AxisListType.X

D = 1024
KC = 8
H = 16
HD = 64
C = 16
NN = 256
DFF = 4096
DH = 512
EPS = 1e-5


class Em:

    ENG = ("tensor", "vector", "scalar", "sync", "gpsimd")

    def __init__(self):
        self.phase = "init"
        self.items = {e: [] for e in self.ENG}
        self.n = {"tensor": 0, "vector": 0, "scalar": 0}
        self.streams = {}
        self.waited = {e: {} for e in self.ENG}
        self.guards = {}

    def dep(self, sem):
        return self.n[sem] if sem in self.n else self.streams.get(sem, 0)

    def emit(self, eng, fn, after=(), tick=False, guard=None, set_guard=None,
             selfw=True):
        assert eng in ("tensor", "vector", "scalar")
        after = list(after)
        if eng in ("vector", "scalar"):
            if selfw and self.n[eng]:
                after.append((eng + "!", self.n[eng]))
            tick = True
        if guard:
            for g in (guard if isinstance(guard, (list, tuple)) else [guard]):
                if g in self.guards:
                    after.append(self.guards[g])
        waits = []
        for p, v in after:
            force = p.endswith("!")
            p = p.rstrip("!")
            if p == eng and not force:
                continue
            have = self.n[p] if p in self.n else self.streams.get(p, 0)
            assert v <= have, f"forward wait {p}>={v} (have {have})"
            if force or self.waited[eng].get(p, 0) < v:
                waits.append((p, v))
                self.waited[eng][p] = v
        if tick:
            self.n[eng] += 1
        if set_guard:
            assert tick
            for g in (set_guard if isinstance(set_guard, (list, tuple))
                      else [set_guard]):
                self.guards[g] = (eng, self.n[eng])
        self.items[eng].append((fn, waits, tick, eng, self.phase))

    def dma(self, eng, out_ap, in_ap, stream, after=(), guard=None,
            set_guard=None):
        assert eng in ("sync", "gpsimd")
        after = list(after)
        if guard:
            for g in (guard if isinstance(guard, (list, tuple)) else [guard]):
                if g in self.guards:
                    after.append(self.guards[g])
        waits = []
        for p, v in after:
            have = self.n[p] if p in self.n else self.streams.get(p, 0)
            assert v <= have, f"forward wait {p}>={v} (have {have})"
            if self.waited[eng].get(p, 0) < v:
                waits.append((p, v))
                self.waited[eng][p] = v
        self.streams[stream] = self.streams.get(stream, 0) + 16
        if set_guard:
            for g in (set_guard if isinstance(set_guard, (list, tuple))
                      else [set_guard]):
                self.guards[g] = (stream, self.streams[stream])
        self.items[eng].append(
            (lambda e, o=out_ap, i=in_ap: e.dma_start(out=o, in_=i),
             waits, True, stream, self.phase))
        return self.streams[stream]

    def wait_all(self, eng, sems):
        for p in sems:
            v = self.dep(p)
            if v and self.waited[eng].get(p, 0) < v:
                self.items[eng].append(
                    (lambda e, p=p, v=v: None, [(p, v)], False, eng, self.phase))
                self.waited[eng][p] = v

    def replay(self, block, sems, nc=None):
        def runner(name):
            def run(eng_obj):
                cur = None
                for fn, waits, tick, target, phase in self.items[name]:
                    if nc is not None and phase != cur:
                        if cur is not None:
                            nc.leave_named_scope(cur, 0, False)
                        nc.enter_named_scope(phase, False)
                        cur = phase
                    for p, v in waits:
                        eng_obj.wait_ge(sems[p], v)
                    inst = fn(eng_obj)
                    if tick and inst is not None:
                        inst.then_inc(sems[target],
                                      16 if target not in self.n else 1)
                if nc is not None and cur is not None:
                    nc.leave_named_scope(cur, 0, False)
            return run
        block.tensor(runner("tensor"))
        block.vector(runner("vector"))
        block.scalar(runner("scalar"))
        block.sync(runner("sync"))
        block.gpsimd(runner("gpsimd"))


def build_program(SEQ, T, detect_races=False, dbg_spill=False):
    NSUP = T // 512
    NTILES = T // 128
    nc = bass.Bass("TRN2", target_bir_lowering=False,
                   detect_race_conditions=detect_races)

    x_d = nc.dram_tensor("x", [SEQ, T, D], F32, kind="ExternalInput").ap()
    wq_d = nc.dram_tensor("wq", [D, D], F32, kind="ExternalInput").ap()
    wk_d = nc.dram_tensor("wk", [D, D], F32, kind="ExternalInput").ap()
    wv_d = nc.dram_tensor("wv", [D, D], F32, kind="ExternalInput").ap()
    wo_d = nc.dram_tensor("wo", [D, D], F32, kind="ExternalInput").ap()
    ce_d = nc.dram_tensor("ce", [C, D], F32, kind="ExternalInput").ap()
    gw1_d = nc.dram_tensor("gw1", [D, DH], F32, kind="ExternalInput").ap()
    gb1_d = nc.dram_tensor("gb1", [DH], F32, kind="ExternalInput").ap()
    gw2_d = nc.dram_tensor("gw2", [DH, C], F32, kind="ExternalInput").ap()
    gb2_d = nc.dram_tensor("gb2", [C], F32, kind="ExternalInput").ap()
    wc_d = nc.dram_tensor("wc", [C, D, NN], F32, kind="ExternalInput").ap()
    bc_d = nc.dram_tensor("bc", [C, NN], F32, kind="ExternalInput").ap()
    pw_d = nc.dram_tensor("pw", [DFF, D], F32, kind="ExternalInput").ap()
    pb_d = nc.dram_tensor("pb", [1, D], F32, kind="ExternalInput").ap()
    lng_d = nc.dram_tensor("lng", [1, D], F32, kind="ExternalInput").ap()
    lnb_d = nc.dram_tensor("lnb", [1, D], F32, kind="ExternalInput").ap()
    out_d = nc.dram_tensor("out", [SEQ, T, D], F32, kind="ExternalOutput").ap()
    spill_kind = "ExternalOutput" if dbg_spill else "Internal"
    x2sp_d = nc.dram_tensor("x2sp", [SEQ, T, D], F32, kind=spill_kind).ap()
    ysp_d = nc.dram_tensor("ysp", [SEQ, T, D], F32, kind=spill_kind).ap()
    nrm_d = nc.dram_tensor("nrm_d", [16, 512], F32, kind="Internal").ap()
    idb_d = nc.inline_tensor(np.eye(128, dtype=ml_dtypes.bfloat16), "idb_c").ap()
    idf_d = nc.inline_tensor(np.eye(128, dtype=np.float32), "idf_c").ap()
    ones_d = nc.inline_tensor(np.ones((1, 128), dtype=np.float32), "ones_c").ap()
    sel_np = np.zeros((C, C * 128), dtype=np.float32)
    for _c in range(C):
        sel_np[_c, _c * 128:(_c + 1) * 128] = 1.0
    sel_d = nc.inline_tensor(sel_np, "sel_c").ap()

    _names = {}
    _reuse = {}

    def _uniq(name):
        k = _names.get(name, 0)
        _names[name] = k + 1
        return name if k == 0 else f"{name}_{k}"

    class _ReuseGuard:

        def __init__(self, inner, key, space):
            self.inner = inner
            self.key = key
            self.space = space

        def __enter__(self):
            addr = nc.bytes_allocated(self.space)
            h = self.inner.__enter__()
            if self.key in _reuse:
                old_addr, old_h = _reuse[self.key]
                assert old_addr == addr, f"layout drift for {self.key}"
                return old_h
            _reuse[self.key] = (addr, h)
            return h

        def __exit__(self, *a):
            return self.inner.__exit__(*a)

    import concourse.bass_types as _bt

    def sb_t(name, shape, dt=F32):
        return _ReuseGuard(nc.sbuf_tensor(_uniq(name), shape, dt),
                           ("sb", name, tuple(shape), dt), bass.MemorySpace.SBUF)

    def ps_t(name, shape, dt=F32):
        return _ReuseGuard(nc.psum_tensor(_uniq(name), shape, dt),
                           ("ps", name, tuple(shape), dt), bass.MemorySpace.PSUM)

    em = Em()
    es = ExitStack()

    with es:
        def sbuf(name, shape, dt=F32):
            return es.enter_context(sb_t(name, shape, dt))

        idb = sbuf("idb", [128, 128], BF16)
        idf = sbuf("idf", [128, 128], F32)
        ones_r = sbuf("ones_r", [1, 128], F32R)
        b1T = sbuf("b1T", [128, 4], F32)
        b2r = sbuf("b2r", [1, C], F32R)
        bcT = sbuf("bcT", [128, 2, C], F32)
        ceT_r = sbuf("ceT_r", [128, KC, C], F32R)
        statb = sbuf("statb", [128, SEQ, NTILES, 4], F32)
        W1r = sbuf("W1r", [128, KC, DH], F32R)
        W2r = sbuf("W2r", [128, 4, C], F32R)


        with ExitStack() as st_es:
            stage = st_es.enter_context(sb_t("stage", [128, DH], F32))
            ones1 = st_es.enter_context(sb_t("ones1", [1, 128], F32))
            stage2 = st_es.enter_context(sb_t("stage2", [16, 1024], F32))
            sc_st = st_es.enter_context(sb_t("sc_st", [16, 1024], F32))
            ssq16 = st_es.enter_context(sb_t("ssq16", [16, 1], F32))
            l16 = st_es.enter_context(sb_t("l16", [16, 1], F32))
            b1s = st_es.enter_context(sb_t("b1s", [4, 128], F32))
            b2s = st_es.enter_context(sb_t("b2s", [1, C], F32))
            bcs = st_es.enter_context(sb_t("bcs", [16, 256], F32))
            em.dma("sync", idb[:], idb_d, stream="ini")
            em.dma("sync", idf[:], idf_d, stream="ini")
            em.dma("sync", ones1[:], ones_d, stream="ini")
            em.dma("sync", b1s[:], gb1_d.rearrange("(c p) -> c p", p=128),
                   stream="ini")
            em.dma("sync", b2s[:], gb2_d.unsqueeze(0), stream="ini")
            em.dma("sync", bcs[:], bc_d, stream="ini")
            em.dma("sync", stage2[:], ce_d, stream="ini")
            ini = em.dep("ini")
            em.emit("vector", lambda e: e.tensor_copy(ones_r[:], ones1[:]),
                    after=[("ini", ini)], tick=True)
            pTi = st_es.enter_context(ps_t("pTi", [128, 128], F32))

            em.emit("tensor", lambda e: e.transpose(pTi[:, 0:4], b1s[:], idf[0:4, 0:4]),
                    after=[("ini", ini)], tick=True)
            em.emit("vector", lambda e: e.tensor_copy(b1T[:], pTi[:, 0:4]),
                    after=[("tensor", em.dep("tensor"))], tick=True)
            em.emit("vector", lambda e: e.tensor_copy(b2r[:], b2s[:]),
                    after=[("ini", ini)], tick=True)
            for half in range(2):
                em.emit("tensor",
                        lambda e, h=half: e.transpose(
                            pTi[:, 32 + 16 * h:48 + 16 * h],
                            bcs[:, 128 * h:128 * h + 128], idf[0:16, 0:16]),
                        after=[("ini", ini), ("vector", em.dep("vector"))], tick=True)
                em.emit("vector",
                        lambda e, h=half: e.tensor_copy(
                            bcT[:, h, :], pTi[:, 32 + 16 * h:48 + 16 * h]),
                        after=[("tensor", em.dep("tensor"))], tick=True)
            em.emit("vector", lambda e: e.scalar_tensor_tensor(
                sc_st[:], stage2[:], 1.0, stage2[:], ALU.mult, ALU.mult,
                accum_out=ssq16[:]), after=[("ini", ini)], tick=True)
            em.emit("vector", lambda e: e.tensor_scalar_max(ssq16[:], ssq16[:], 1e-30),
                    after=[("vector!", em.dep("vector"))], tick=True)
            em.emit("scalar", lambda e: e.activation(l16[:], ssq16[:], AF.Ln),
                    after=[("vector", em.dep("vector"))], tick=True)
            em.emit("scalar", lambda e: e.activation(l16[:], l16[:], AF.Exp, scale=-0.5),
                    after=[("scalar!", em.dep("scalar"))], tick=True)
            em.emit("vector", lambda e: e.tensor_scalar_mul(stage2[:], stage2[:], l16[:]),
                    after=[("scalar", em.dep("scalar"))], tick=True)
            vq = em.dep("vector")
            for kc in range(KC):
                em.emit("tensor", lambda e, k=kc: e.transpose(
                    pTi[:, 64:80], stage2[:, 128 * k:128 * k + 128], idf[0:16, 0:16]),
                    after=[("vector", vq)], guard="pTi2", tick=True)
                em.emit("vector", lambda e, k=kc: e.tensor_copy(ceT_r[:, k, :],
                                                                pTi[:, 64:80]),
                        after=[("tensor", em.dep("tensor"))], tick=True,
                        set_guard="pTi2")
            for kc in range(KC):
                v = em.dma("sync", stage[:], gw1_d[128 * kc:128 * kc + 128, :],
                           stream="stg", guard="stage")
                em.emit("vector", lambda e, k=kc: e.tensor_copy(W1r[:, k, :], stage[:]),
                        after=[("stg", v)], tick=True, set_guard="stage")
            for kc in range(4):
                v = em.dma("sync", stage[:, 0:C], gw2_d[128 * kc:128 * kc + 128, :],
                           stream="stg", guard="stage")
                em.emit("vector", lambda e, k=kc: e.tensor_copy(W2r[:, k, :],
                                                                stage[:, 0:C]),
                        after=[("stg", v)], tick=True, set_guard="stage")
            init_v = em.dep("vector")
            init_t = em.dep("tensor")
            init_s = em.dep("scalar")

        with ExitStack() as a_es:
            def asb(name, shape, dt=F32):
                return a_es.enter_context(sb_t(name, shape, dt))
            Wq_sb = asb("Wq_sb", [128, KC, D], BF16)
            Wk_sb = asb("Wk_sb", [128, KC, D], BF16)
            Wv_sb = asb("Wv_sb", [128, KC, D], BF16)
            Wo_sb = asb("Wo_sb", [128, KC, D], BF16)
            k_f = asb("k_f", [128, KC, T], BF16)
            v_on = asb("v_on", [128, NTILES, H, 65], BF16)
            x1b = asb("x1b", [128, 4, D], F32)
            q_f = asb("q_f", [128, KC, 512], BF16)
            att_f = asb("att_f", [128, KC, 512], BF16)
            x2b = asb("x2b", [128, 2, D], F32)
            sqscr = asb("sqscr", [128, 512], F32)
            ssqN2 = asb("ssqN2", [128, 16], F32)
            ssqN = asb("ssqN", [128, 16], F32)
            smN = asb("smN", [128, 16], F32)
            m2N = asb("m2N", [128, 16], F32)
            tvN = asb("tvN", [128, 16], F32)
            rvN = asb("rvN", [128, 16], F32)
            vaN = asb("vaN", [128, 16], F32)
            ltN = asb("ltN", [128, 16], F32)

            for w_sb, w_d in ((Wq_sb, wq_d), (Wk_sb, wk_d), (Wv_sb, wv_d),
                              (Wo_sb, wo_d)):
                em.dma("gpsimd", w_sb[:], w_d.rearrange("(kc p) o -> p kc o", p=128),
                       stream="wld", after=[("vector", init_v), ("tensor", init_t)])
            wload = em.dep("wld")

            def ln_big(x_ap, col, after):
                em.emit("vector", lambda e: e.scalar_tensor_tensor(
                    sqscr[:], x_ap[:, 0:512], 1.0, x_ap[:, 0:512], ALU.mult,
                    ALU.mult, accum_out=ssqN[:, col:col + 1]),
                    after=after, tick=True)
                em.emit("vector", lambda e: e.scalar_tensor_tensor(
                    sqscr[:], x_ap[:, 512:1024], 1.0, x_ap[:, 512:1024], ALU.mult,
                    ALU.mult, accum_out=ssqN2[:, col:col + 1]), tick=True)
                em.emit("vector", lambda e: e.reduce_sum(
                    smN[:, col:col + 1], x_ap, axis=AX), tick=True)
                return em.dep("vector")

            def ln_tail(s, t0, nt, want_invn):
                negm = statb[:, s, t0:t0 + nt, 0]
                rstd = statb[:, s, t0:t0 + nt, 1]
                nb = statb[:, s, t0:t0 + nt, 2]
                invn = statb[:, s, t0:t0 + nt, 3]
                em.emit("vector", lambda e: e.tensor_add(
                    ssqN[:, 0:nt], ssqN[:, 0:nt], ssqN2[:, 0:nt]),
                    guard="lnscr", tick=True)
                em.emit("vector", lambda e: e.tensor_scalar(
                    tvN[:, 0:nt], ssqN[:, 0:nt], 1.0 / D, EPS, ALU.mult, ALU.add),
                    tick=True)
                em.emit("vector", lambda e: e.tensor_scalar_mul(
                    negm, smN[:, 0:nt], -1.0 / D), tick=True)
                em.emit("vector", lambda e: e.tensor_mul(m2N[:, 0:nt], negm, negm),
                        tick=True)
                em.emit("vector", lambda e: e.tensor_sub(
                    tvN[:, 0:nt], tvN[:, 0:nt], m2N[:, 0:nt]), tick=True)
                vq = em.dep("vector")
                em.emit("scalar", lambda e: e.activation(ltN[:, 0:nt], tvN[:, 0:nt],
                                                         AF.Ln),
                        after=[("vector", vq), ("scalar!", em.dep("scalar"))],
                        tick=True)
                em.emit("scalar", lambda e: e.activation(rstd, ltN[:, 0:nt], AF.Exp,
                                                         scale=-0.5),
                        after=[("scalar!", em.dep("scalar"))], tick=True,
                        set_guard=(None if want_invn else "lnscr"))
                em.emit("vector", lambda e: e.tensor_mul(nb, negm, rstd),
                        after=[("scalar", em.dep("scalar"))], tick=True)
                if want_invn:
                    em.emit("vector", lambda e: e.reciprocal(rvN[:, 0:nt],
                                                             tvN[:, 0:nt]), tick=True)
                    em.emit("vector", lambda e: e.tensor_scalar_add(
                        vaN[:, 0:nt], tvN[:, 0:nt], -EPS), tick=True)
                    em.emit("vector", lambda e: e.tensor_mul(
                        vaN[:, 0:nt], vaN[:, 0:nt], rvN[:, 0:nt]), tick=True)
                    em.emit("vector", lambda e: e.tensor_scalar(
                        vaN[:, 0:nt], vaN[:, 0:nt], float(D), 1e-24, ALU.mult,
                        ALU.max), tick=True)
                    vq = em.dep("vector")
                    em.emit("scalar", lambda e: e.activation(ltN[:, 0:nt],
                                                             vaN[:, 0:nt], AF.Ln),
                            after=[("vector", vq),
                                   ("scalar!", em.dep("scalar"))], tick=True)
                    em.emit("scalar", lambda e: e.activation(invn, ltN[:, 0:nt],
                                                             AF.Exp, scale=-0.5),
                            after=[("scalar!", em.dep("scalar"))], tick=True,
                            set_guard="lnscr")
                return em.dep("vector"), em.dep("scalar")

            for s in range(SEQ):
                em.phase = f"A{s}"
                em.emit("vector", lambda e: e.memset(v_on[:, :, :, 64:65], 1.0),
                        after=[("tensor", em.dep("tensor"))], tick=True)
                vones = em.dep("vector")

                for u in range(NSUP):
                    with ExitStack() as ps_es:
                        xn1f = ps_es.enter_context(
                            sb_t("xn1f", [128, KC, 512], BF16))
                        xn1tm = ps_es.enter_context(
                            sb_t("xn1tm", [128, 2, D], BF16))
                        vtmp = ps_es.enter_context(sb_t("vtmp", [128, 512], BF16))
                        pA = [ps_es.enter_context(
                            ps_t(f"pA{i}", [128, 512], F32)) for i in range(3)]
                        pTb = [ps_es.enter_context(
                            ps_t(f"pTb{i}", [128, 128], BF16)) for i in range(4)]

                        em.phase = f"qkv{s}_{u}"
                        qev_kc = {}
                        xvs = []
                        for i in range(4):
                            xvs.append(em.dma(
                                "sync", x1b[:, i, :],
                                x_d[s, 512 * u + 128 * i:512 * u + 128 * i + 128, :],
                                stream=f"x1{i}", guard=f"x1b{i}",
                                after=[("scalar", em.dep("scalar"))]))
                        for i in range(4):
                            ln_big(x1b[:, i, :], i, [(f"x1{i}", xvs[i])])
                        vq, sq0 = ln_tail(s, 4 * u, 4, True)
                        for i in range(4):
                            t = 4 * u + i
                            em.emit("scalar", lambda e, i=i, s=s, t=t: e.activation(
                                xn1tm[:, i % 2, :], x1b[:, i, :], AF.Identity,
                                bias=statb[:, s, t, 2:3], scale=statb[:, s, t, 1:2]),
                                after=[("vector", vq), ("scalar", sq0)],
                                guard=f"xn1tm{i % 2}",
                                tick=True)
                            sq = em.dep("scalar")
                            for kc in range(KC):
                                em.emit("tensor", lambda e, i=i, kc=kc: e.transpose(
                                    pTb[kc % 4][:],
                                    xn1tm[:, i % 2, 128 * kc:128 * kc + 128], idb[:]),
                                    after=[("scalar", sq), ("ini", 112)],
                                    guard=f"pTb{kc % 4}", tick=True,
                                    set_guard=(f"xn1tm{i % 2}" if kc == KC - 1 else None))
                                em.emit("vector", lambda e, i=i, kc=kc: e.tensor_copy(
                                    xn1f[:, kc, 128 * i:128 * i + 128], pTb[kc % 4][:]),
                                    after=[("tensor", em.dep("tensor"))], tick=True,
                                    selfw=False, set_guard=f"pTb{kc % 4}")
                                qev_kc[kc] = em.dep("vector")
                        xn1_done = em.dep("vector")

                        for oc in range(KC):
                            for wi, wsb in enumerate((Wq_sb, Wk_sb, Wv_sb)):
                                pb = pA[wi]
                                for kc in range(KC):
                                    em.emit("tensor",
                                            lambda e, wsb=wsb, kc=kc, oc=oc, pb=pb: e.matmul(
                                                pb[:],
                                                wsb[:, kc, 128 * oc:128 * oc + 128],
                                                xn1f[:, kc, :], start=(kc == 0),
                                                stop=(kc == KC - 1)),
                                            after=[("vector", xn1_done),
                                                   ("wld", wload)],
                                            guard=(f"pA{wi}" if kc == 0 else None),
                                            tick=(kc == KC - 1))
                                tq = em.dep("tensor")
                                if wi == 0:
                                    em.emit("vector", lambda e, oc=oc, pb=pb: e.tensor_copy(
                                        q_f[:, oc, :], pb[:]), after=[("tensor", tq)],
                                        tick=True, selfw=False, set_guard=f"pA{wi}")
                                elif wi == 1:
                                    em.emit("vector",
                                            lambda e, oc=oc, u=u, pb=pb: e.tensor_copy(
                                                k_f[:, oc, 512 * u:512 * u + 512], pb[:]),
                                            after=[("tensor", tq)], tick=True,
                                            selfw=False, set_guard=f"pA{wi}")
                                else:
                                    em.emit("vector", lambda e, pb=pb: e.tensor_copy(
                                        vtmp[:], pb[:]), after=[("tensor", tq)],
                                        tick=True, set_guard=f"pA{wi}")
                                    vq = em.dep("vector")
                                    for j in range(4):
                                        em.emit("tensor", lambda e, j=j: e.transpose(
                                            pTb[j][:], vtmp[:, 128 * j:128 * j + 128],
                                            idb[:]),
                                            after=[("vector", vq)],
                                            guard=f"pTb{j}", tick=True)
                                        dstv = v_on[:, 4 * u + j, 2 * oc:2 * oc + 2, 0:64]
                                        em.emit("vector",
                                                lambda e, j=j, dstv=dstv: e.tensor_copy(
                                                    dstv,
                                                    pTb[j][:].rearrange(
                                                        "p (h c) -> p h c", h=2)),
                                                after=[("tensor", em.dep("tensor")),
                                                       ("vector", vones)],
                                                tick=True, selfw=False,
                                                set_guard=f"pTb{j}")
                        qkv_v = em.dep("vector")

                    with ExitStack() as aw_es:
                        em.phase = f"attn{s}_{u}"
                        nrm = aw_es.enter_context(sb_t("nrm", [16, 512], F32))
                        rbw = aw_es.enter_context(sb_t("rbw", [128, 2, 512], F32))
                        scr = aw_es.enter_context(sb_t("scr", [1, 512], F32))
                        tt_ticks = {}

                        def norm_chain():
                            em.emit("vector", lambda e: e.reciprocal(
                                nrm[:], nrm[:]),
                                after=[("nrb0", em.dep("nrb0")),
                                       ("nrb1", em.dep("nrb1"))], tick=True)
                            nd = em.dma("gpsimd", nrm_d[:], nrm[:],
                                        stream="nrb0",
                                        after=[("vector", em.dep("vector"))])
                            for kc in range(KC):
                                gds = []
                                for i2 in range(2):
                                    st = f"nrb{2 + 2 * (kc % 2) + i2}"
                                    gds.append((st, em.dma(
                                        "gpsimd" if i2 == 0 else "sync",
                                        rbw[64 * i2:64 * i2 + 64, kc % 2, :],
                                        nrm_d[2 * kc + i2, None, :].broadcast_to(
                                            (64, 512)),
                                        stream=st,
                                        guard=f"rbw{kc % 2}",
                                        after=[("nrb0", nd)])))
                                em.emit("vector", lambda e, kc=kc: e.tensor_tensor(
                                    att_f[:, kc, :], att_f[:, kc, :],
                                    rbw[:, kc % 2, :],
                                    ALU.mult),
                                    after=gds, tick=True,
                                    set_guard=f"rbw{kc % 2}")
                                tt_ticks[kc] = em.dep("vector")

                        with ExitStack() as ps_es:
                            PT = ps_es.enter_context(
                                sb_t("PT", [128, 3, 1536], BF16))
                            pSc = [ps_es.enter_context(
                                ps_t(f"pSc{i}", [128, 1536], F32))
                                for i in range(2)]
                            pU = [ps_es.enter_context(
                                ps_t(f"pU{i}", [65, 512], F32)) for i in range(2)]
                            items = [(j, 0) for j in range(4 * u)]
                            items += [(4 * u + jl, 128 * jl)
                                      for jl in (0, 1, 3, 2)]
                            slots = []
                            cur, col = [], 0
                            for j, off in items:
                                n = 512 - off
                                c = col
                                if c // 512 != (c + n - 1) // 512:
                                    c = (c // 512 + 1) * 512
                                if c + n > 1536:
                                    slots.append((cur, col))
                                    cur, c = [], 0
                                cur.append((j, off, c))
                                col = c + n
                            slots.append((cur, col))
                            ng = len(slots)

                            def make_tail(h, pu):
                                oc, r0 = h // 2, (h % 2) * 64
                                av_t = em.dep("tensor")

                                def tail():
                                    em.emit("vector", lambda e: e.tensor_copy(
                                        scr[:], pu[64:65, :]),
                                        after=[("tensor", av_t)],
                                        guard="scr0", tick=True)
                                    em.dma("gpsimd", nrm[h:h + 1, :],
                                           scr[:],
                                           stream=f"nrb{h % 2}",
                                           after=[("vector", em.dep("vector"))],
                                           set_guard="scr0")
                                    em.emit("vector", lambda e: e.tensor_copy(
                                        att_f[r0:r0 + 64, oc, :], pu[0:64, :]),
                                        tick=True, set_guard=f"pU{h % 2}")
                                return tail

                            exp_ticks = {}
                            tails = {}
                            flat = [(h, gi) for h in range(H) for gi in range(ng)]
                            nfl = len(flat)
                            LAG = 2
                            for t_ in range(nfl + LAG):
                                if t_ < nfl:
                                    h, gi = flat[t_]
                                    oc, r0 = h // 2, (h % 2) * 64
                                    grp, wid = slots[gi]
                                    for j, off, col in grp:
                                        n = 512 - off
                                        em.emit("tensor",
                                                lambda e, j=j, off=off, n=n, col=col, oc=oc, r0=r0, t_=t_: e.matmul(
                                                    pSc[t_ % 2][:, col:col + n],
                                                    k_f[r0:r0 + 64, oc, 128 * j:128 * j + 128],
                                                    q_f[r0:r0 + 64, oc, off:512],
                                                    start=True, stop=True),
                                                after=[("vector", qkv_v)],
                                                guard=(f"pSc{t_ % 2}" if col == 0
                                                       else None), tick=True)
                                    em.emit("scalar", lambda e, wid=wid, t_=t_: e.activation(
                                        PT[:, t_ % 3, 0:wid], pSc[t_ % 2][:, 0:wid],
                                        AF.Exp, scale=0.125),
                                        after=[("tensor", em.dep("tensor"))],
                                        tick=True, selfw=False,
                                        set_guard=f"pSc{t_ % 2}")
                                    exp_ticks[t_] = em.dep("scalar")
                                if t_ - LAG in tails:
                                    tails.pop(t_ - LAG)[1]()
                                if t_ >= LAG:
                                    ai = t_ - LAG
                                    h, gi = flat[ai]
                                    pu = pU[h % 2]
                                    grp, wid = slots[gi]
                                    for mi, (j, off, col) in enumerate(grp):
                                        n = 512 - off
                                        first = (gi == 0 and mi == 0)
                                        last = (gi == ng - 1 and mi == len(grp) - 1)
                                        em.emit("tensor",
                                                lambda e, j=j, off=off, n=n, col=col, h=h, ai=ai, pu=pu, first=first, last=last: e.matmul(
                                                    pu[:, off:512], v_on[:, j, h, :],
                                                    PT[:, ai % 3, col:col + n],
                                                    start=first, stop=last),
                                                after=[("scalar", exp_ticks[ai])],
                                                guard=(f"pU{h % 2}" if first
                                                       else None),
                                                tick=last)
                                    if gi == ng - 1:
                                        tails[t_] = (h, make_tail(h, pu))
                            for k_ in sorted(tails):
                                tails.pop(k_)[1]()
                            norm_chain()

                        with ExitStack() as ps_es:
                            em.phase = f"wo{s}_{u}"
                            pW = [ps_es.enter_context(
                                ps_t(f"pW{i}", [128, 512], F32)) for i in range(8)]
                            mm_ticks = {}
                            for kc in range(KC):
                                for i in range(4):
                                    for oh in range(2):
                                        pb = pW[2 * i + oh]
                                        em.emit("tensor",
                                                lambda e, i=i, oh=oh, kc=kc, pb=pb: e.matmul(
                                                    pb[:],
                                                    att_f[:, kc, 128 * i:128 * i + 128],
                                                    Wo_sb[:, kc, 512 * oh:512 * oh + 512],
                                                    start=(kc == 0), stop=(kc == KC - 1)),
                                                after=[("vector", tt_ticks[kc])],
                                                guard=(f"pW{2 * i + oh}" if kc == 0
                                                       else None),
                                                tick=(kc == KC - 1))
                                        if kc == KC - 1:
                                            mm_ticks[2 * i + oh] = em.dep("tensor")
                            for i in range(4):
                                t = 4 * u + i
                                for oh in range(2):
                                    pb = pW[2 * i + oh]
                                    em.emit("vector",
                                            lambda e, i=i, oh=oh, pb=pb: e.tensor_add(
                                                x2b[:, i % 2, 512 * oh:512 * oh + 512],
                                                pb[:],
                                                x1b[:, i, 512 * oh:512 * oh + 512]),
                                            after=[("tensor", mm_ticks[2 * i + oh])],
                                            guard=(f"x2b{i % 2}" if oh == 0 else None),
                                            tick=True,
                                            set_guard=([f"pW{2 * i + oh}"]
                                                       + ([f"x1b{i}"] if oh == 1
                                                          else [])))
                                vq = em.dep("vector")
                                em.dma("gpsimd", x2sp_d[s, 128 * t:128 * t + 128, :],
                                       x2b[:, i % 2, :], stream=f"x2o{i % 2}",
                                       after=[("vector", vq)],
                                       set_guard=f"x2b{i % 2}")
                                ln_big(x2b[:, i % 2, :], i, [("vector", vq)])
                            ln_tail(s, 4 * u, 4, True)
            a_t, a_v, a_s = em.dep("tensor"), em.dep("vector"), em.dep("scalar")
            a_sp = [("x2o0", em.dep("x2o0")), ("x2o1", em.dep("x2o1"))]

        with ExitStack() as b_es:
            def bsb(name, shape, dt=F32):
                return b_es.enter_context(sb_t(name, shape, dt))
            Wcb = bsb("Wcb", [128, KC, DFF], BF16)
            pwb = bsb("pwb", [128, 32, D], BF16)
            g_bc = bsb("g_bc", [128, D], F32)
            b_bc = bsb("b_bc", [128, D], F32)
            x2l = bsb("x2l", [128, 2, D], F32)
            wbuf = bsb("wbuf", [128, NTILES, C], F32)
            ybuf = bsb("ybuf", [128, 2, D], F32)
            bsm = bsb("bsm", [128, 1], F32)
            btv = bsb("btv", [128, 1], F32)
            bm2 = bsb("bm2", [128, 1], F32)
            bltv = bsb("bltv", [128, 1], F32)
            brstd = bsb("brstd", [128, 1], F32)
            bnb = bsb("bnb", [128, 1], F32)

            em.phase = "mid"
            mid_after = [("tensor", a_t), ("vector", a_v), ("scalar", a_s),
                         ("x2o0", em.dep("x2o0")), ("x2o1", em.dep("x2o1")),
                         ("x10", em.dep("x10")), ("x11", em.dep("x11")),
                         ("x12", em.dep("x12")), ("x13", em.dep("x13")),
                         ("wld", em.dep("wld")),
                         ("nrb0", em.dep("nrb0")), ("nrb1", em.dep("nrb1")),
                         ("nrb2", em.dep("nrb2")), ("nrb3", em.dep("nrb3")),
                         ("nrb4", em.dep("nrb4")), ("nrb5", em.dep("nrb5"))]
            for kc in range(KC):
                em.dma("gpsimd", Wcb[:, kc, :].rearrange("p (c n) -> p c n", c=C),
                       wc_d.rearrange("c (kc p) n -> p kc c n", p=128)[:, kc, :, :],
                       stream="midg", after=mid_after)
            for g in range(4):
                em.dma("gpsimd", pwb[:, 8 * g:8 * g + 8, :],
                       pw_d.rearrange("(nc p) o -> p nc o", p=128)[:, 8 * g:8 * g + 8, :],
                       stream="midg", after=mid_after)

            em.dma("sync", g_bc[:], lng_d.broadcast_to((128, D)), stream="mids",
                   after=mid_after)
            em.dma("sync", b_bc[:], lnb_d.broadcast_to((128, D)), stream="mids",
                   after=mid_after)
            mid_g = em.dep("midg")
            mid_s = em.dep("mids")

            def gen_b1(s):
                em.phase = f"b1_{s}"
                with ExitStack() as b1_es:
                    h2buf = b1_es.enter_context(sb_t(f"h2buf@{s}", [128, NTILES, C], F32))
                    simbuf = b1_es.enter_context(sb_t(f"simbuf@{s}", [128, NTILES, C], F32))
                    gbuf = b1_es.enter_context(sb_t(f"gbuf@{s}", [128, NTILES, C], F32))
                    lsc = b1_es.enter_context(sb_t(f"lsc@{s}", [128, C], F32))
                    lgt = b1_es.enter_context(sb_t(f"lgt@{s}", [128, C], F32))
                    emx = b1_es.enter_context(sb_t(f"emx@{s}", [128, 1], F32))
                    esum = b1_es.enter_context(sb_t(f"esum@{s}", [128, 1], F32))
                    erin = b1_es.enter_context(sb_t(f"erin@{s}", [128, 1], F32))
                    emsk = b1_es.enter_context(sb_t(f"emsk@{s}", [128, C], F32))
                    eexp = b1_es.enter_context(sb_t(f"eexp@{s}", [128, C], F32))
                    xn2tm = b1_es.enter_context(
                        sb_t(f"xn2tm@{s}", [128, 2, D], F32))
                    xnr = b1_es.enter_context(
                        sb_t(f"xnr@{s}", [128, KC, 256], F32R))
                    h1scr = b1_es.enter_context(
                        sb_t(f"h1scr@{s}", [128, 256], F32))
                    h1r = b1_es.enter_context(
                        sb_t(f"h1r@{s}", [128, 4, 256], F32R))
                    pG = [b1_es.enter_context(
                        ps_t(f"pG{i}@{s}", [128, 256], F32)) for i in range(4)]
                    pSG = b1_es.enter_context(ps_t(f"pSG@{s}", [128, 64], F32))
                    pTf = [b1_es.enter_context(
                        ps_t(f"pTf{i}@{s}", [128, 128], F32)) for i in range(3)]
                    b1sv, b1ss = em.dep("vector"), em.dep("scalar")
                    for u in range(NTILES // 2):
                        em.phase = f"b1_{s}"
                        for i in range(2):
                            t = 2 * u + i
                            ld = em.dma(
                                "gpsimd", x2l[:, i, :],
                                x2sp_d[s, 128 * t:128 * t + 128, :],
                                stream=f"x2g{i}",
                                after=a_sp + [("vector", b1sv),
                                              ("scalar", b1ss)],
                                guard=f"x2l{i}")
                            em.emit("scalar", lambda e, i=i, s=s, t=t: e.activation(
                                xn2tm[:, i, :], x2l[:, i, :], AF.Identity,
                                bias=statb[:, s, t, 2:3], scale=statb[:, s, t, 1:2]),
                                after=[(f"x2g{i}", ld), ("vector", a_v)],
                                guard=f"xn2tm{i}", tick=True,
                                set_guard=f"x2l{i}")
                        apply_s = em.dep("scalar")
                        ev_kc = {}
                        for kc in range(KC):
                            for i in range(2):
                                em.emit("tensor", lambda e, kc=kc, i=i: e.transpose(
                                    pTf[(2 * kc + i) % 3][:],
                                    xn2tm[:, i, 128 * kc:128 * kc + 128], idf[:]),
                                    after=[("scalar", apply_s)],
                                    guard=f"pTf{(2 * kc + i) % 3}", tick=True,
                                    set_guard=(f"xn2tm{i}" if kc == KC - 1 else None))
                                em.emit("vector", lambda e, kc=kc, i=i: e.tensor_copy(
                                    xnr[:, kc, 128 * i:128 * i + 128],
                                    pTf[(2 * kc + i) % 3][:]),
                                    after=[("tensor", em.dep("tensor"))],
                                    guard=("xnr" if (kc == 0 and i == 0) else None),
                                    tick=True, selfw=False,
                                    set_guard=f"pTf{(2 * kc + i) % 3}")
                            ev_kc[kc] = em.dep("vector")
                        for kc in range(KC):
                            for oc in range(4):
                                em.emit("tensor", lambda e, kc=kc, oc=oc: e.matmul(
                                    pG[oc][:], W1r[:, kc, 128 * oc:128 * oc + 128],
                                    xnr[:, kc, :], start=(kc == 0),
                                    stop=(kc == KC - 1)),
                                    after=[("vector", ev_kc[kc])],
                                    guard=(f"pG{oc}" if kc == 0 else None),
                                    tick=(kc == KC - 1))
                        for kc in range(KC):
                            for i in range(2):
                                em.emit("tensor", lambda e, kc=kc, i=i: e.matmul(
                                    pSG[:, 16 * i:16 * i + 16],
                                    xnr[:, kc, 128 * i:128 * i + 128],
                                    ceT_r[:, kc, :],
                                    start=(kc == 0 and i == 0),
                                    stop=(kc == KC - 1 and i == 1)),
                                    after=[("vector", ev_kc[kc])],
                                    guard=("pSG" if (kc == 0 and i == 0) else None),
                                    tick=(kc == KC - 1 and i == 1),
                                    set_guard=("xnr" if (kc == KC - 1 and i == 1)
                                               else None))
                        g1_t = em.dep("tensor")
                        for i in range(2):
                            t = 2 * u + i
                            em.emit("vector", lambda e, i=i, t=t: e.tensor_copy(
                                simbuf[:, t, :], pSG[:, 16 * i:16 * i + 16]),
                                after=[("tensor", g1_t)], tick=True, selfw=False,
                                set_guard=("pSG_sim" if i == 1 else None))
                        for oc in range(4):
                            em.emit("scalar", lambda e, oc=oc: e.activation(
                                h1scr[:], pG[oc][:], AF.Gelu, bias=b1T[:, oc:oc + 1]),
                                after=[("tensor", g1_t)], guard="h1scr", tick=True,
                                set_guard=f"pG{oc}")
                            em.emit("vector", lambda e, oc=oc: e.tensor_copy(
                                h1r[:, oc, :], h1scr[:]),
                                after=[("scalar", em.dep("scalar"))],
                                guard=("h1r" if oc == 0 else None),
                                tick=True, selfw=False, set_guard="h1scr")
                        h1_v = em.dep("vector")
                        for i in range(2):
                            for kc in range(4):
                                em.emit("tensor", lambda e, i=i, kc=kc: e.matmul(
                                    pSG[:, 32 + 16 * i:48 + 16 * i],
                                    h1r[:, kc, 128 * i:128 * i + 128], W2r[:, kc, :],
                                    start=(kc == 0 and i == 0), stop=False),
                                    after=[("vector", h1_v)],
                                    guard=("pSG_sim" if (kc == 0 and i == 0)
                                           else None))
                            em.emit("tensor", lambda e, i=i: e.matmul(
                                pSG[:, 32 + 16 * i:48 + 16 * i], ones_r[:], b2r[:],
                                start=False, stop=(i == 1)),
                                tick=(i == 1),
                                set_guard=("h1r" if i == 1 else None))
                        g2_t = em.dep("tensor")
                        for i in range(2):
                            t = 2 * u + i
                            em.emit("vector", lambda e, i=i, t=t: e.tensor_copy(
                                h2buf[:, t, :], pSG[:, 32 + 16 * i:48 + 16 * i]),
                                after=[("tensor", g2_t)], tick=True, selfw=False,
                                set_guard=("pSG" if i == 1 else None))
                        yield
                    em.phase = f"b1d_{s}"
                    vq = em.dep("vector")
                    em.emit("scalar", lambda e: e.activation(
                        gbuf[:].rearrange("p a b -> p (a b)"),
                        h2buf[:].rearrange("p a b -> p (a b)"), AF.Sigmoid),
                        after=[("vector", vq)], tick=True)
                    sig_s = em.dep("scalar")
                    for t in range(NTILES):
                        em.emit("vector", lambda e, s=s, t=t: e.tensor_scalar_mul(
                            lsc[:], simbuf[:, t, :], statb[:, s, t, 3:4]),
                            after=[("scalar", sig_s)], tick=True)
                        em.emit("vector", lambda e, t=t: e.tensor_add(
                            lgt[:], lsc[:], gbuf[:, t, :]), guard="lgt", tick=True)
                        em.emit("vector", lambda e: e.reduce_max(
                            emx[:], lgt[:], axis=AX, negate=True), tick=True)
                        vq = em.dep("vector")
                        em.emit("scalar", lambda e: e.activation(
                            eexp[:], lgt[:], AF.Exp, bias=emx[:], accum_out=esum[:]),
                            after=[("vector", vq)], tick=True)
                        sq = em.dep("scalar")
                        em.emit("vector", lambda e: e.reciprocal(erin[:], esum[:]),
                                after=[("scalar", sq)], tick=True)
                        em.emit("vector", lambda e: e.tensor_scalar(
                            emsk[:], lgt[:], emx[:], 0.0, ALU.add, ALU.is_ge),
                            tick=True)
                        em.emit("vector", lambda e: e.tensor_mul(
                            emsk[:], emsk[:], eexp[:]), tick=True, set_guard="lgt")
                        em.emit("vector", lambda e, t=t: e.tensor_scalar_mul(
                            wbuf[:, t, :], emsk[:], erin[:]),
                            guard=("wbufr" if t == 0 else None), tick=True)
                    yield

            def b2_phase(s, w_v):
                em.phase = f"b2_{s}"
                with ExitStack() as b2_es:
                    selr = b2_es.enter_context(
                        sb_t(f"selr@{s}", [C, C * 128], BF16))
                    pb_bc = b2_es.enter_context(
                        sb_t(f"pb_bc@{s}", [128, D], F32))
                    selg = em.dma("gpsimd", selr[:], sel_d, stream="midg",
                                  after=[("midg", em.dep("midg")),
                                         ("tensor", em.dep("tensor")),
                                         ("vector", em.dep("vector")),
                                         ("scalar", em.dep("scalar"))])
                    pbg = em.dma("sync", pb_bc[:],
                                 pb_d.broadcast_to((128, D)), stream="mids",
                                 after=[("mids", em.dep("mids")),
                                        ("tensor", em.dep("tensor")),
                                        ("vector", em.dep("vector")),
                                        ("scalar", em.dep("scalar"))])
                    xn2tmb = b2_es.enter_context(
                        sb_t(f"xn2tmb@{s}", [128, 2, D], BF16))
                    wcm = b2_es.enter_context(sb_t(f"wcm@{s}", [C, 256], BF16))
                    xn2f = b2_es.enter_context(
                        sb_t(f"xn2f@{s}", [128, KC, 256], BF16))
                    colsw = b2_es.enter_context(
                        sb_t(f"colsw@{s}", [128, 32, 256], BF16))
                    gcol = b2_es.enter_context(
                        sb_t(f"gcol@{s}", [128, 2, 256], F32))
                    pCc = [b2_es.enter_context(
                        ps_t(f"pCc{i}@{s}", [128, 256], F32)) for i in range(3)]
                    pWb = [b2_es.enter_context(
                        ps_t(f"pWb{i}@{s}", [128, 256], F32)) for i in range(2)]
                    pP = [b2_es.enter_context(
                        ps_t(f"pP{i}@{s}", [128, 512], F32)) for i in range(2)]
                    pTc = [pP[i][:].bitcast(BF16)[:, 0:128] for i in range(2)]
                    def b2_prelude(tb, s=s):
                        for i in range(2):
                            t = 2 * tb + i
                            ld = em.dma(
                                "sync", x2l[:, i, :],
                                x2sp_d[s, 128 * t:128 * t + 128, :],
                                stream=f"x2l{i}", guard=f"x2l{i}",
                                after=[("vector", w_v),
                                       ("scalar", em.dep("scalar"))])
                            em.emit("scalar", lambda e, i=i, s=s, t=t: e.activation(
                                xn2tmb[:, i, :], x2l[:, i, :], AF.Identity,
                                bias=statb[:, s, t, 2:3], scale=statb[:, s, t, 1:2]),
                                after=[(f"x2l{i}", ld), ("midg", mid_g),
                                       ("mids", mid_s)],
                                guard=f"xn2tmb{i}", tick=True)
                        apply_s = em.dep("scalar")
                        for kc in range(KC):
                            for i in range(2):
                                em.emit("tensor", lambda e, kc=kc, i=i: e.transpose(
                                    pTc[i], xn2tmb[:, i, 128 * kc:128 * kc + 128],
                                    idb[:]), after=[("scalar", apply_s)],
                                    guard=[f"pP{i}", "xn2f"], tick=True,
                                    set_guard=(f"xn2tmb{i}" if kc == KC - 1 else None))
                                em.emit("vector", lambda e, kc=kc, i=i: e.tensor_copy(
                                    xn2f[:, kc, 128 * i:128 * i + 128], pTc[i]),
                                    after=[("tensor", em.dep("tensor"))],
                                    tick=True, selfw=False, set_guard=f"pP{i}")
                        for i in range(2):
                            t = 2 * tb + i
                            em.emit("tensor", lambda e, t=t: e.transpose(
                                pCc[0][0:16, 0:128], wbuf[:, t, :], idf[:]),
                                after=[("vector", w_v)], guard="pCc0", tick=True,
                                set_guard="wbufr")
                            em.emit("vector", lambda e, i=i: e.tensor_copy(
                                wcm[:, 128 * i:128 * i + 128], pCc[0][0:16, 0:128]),
                                after=[("tensor", em.dep("tensor"))],
                                guard=("wcm" if i == 0 else None),
                                tick=True, set_guard="pCc0")
                        return em.dep("vector")

                    tr_v = b2_prelude(0)
                    for tb in range(NTILES // 2):
                        colsw_v = {}
                        for ncc in range(32):
                            c, half = ncc // 2, ncc % 2
                            for kc in range(KC):
                                em.emit("tensor", lambda e, ncc=ncc, kc=kc: e.matmul(
                                    pCc[ncc % 3][:],
                                    Wcb[:, kc, 128 * ncc:128 * ncc + 128],
                                    xn2f[:, kc, :], start=(kc == 0),
                                    stop=(kc == KC - 1)),
                                    after=[("vector", tr_v), ("midg", selg)],
                                    guard=(f"pCc{ncc % 3}" if kc == 0 else None),
                                    tick=(kc == KC - 1))
                            tq = em.dep("tensor")
                            em.emit("tensor", lambda e, c=c, ncc=ncc: e.matmul(
                                pWb[ncc % 2][:], selr[:, 128 * c:128 * c + 128],
                                wcm[:],
                                start=True, stop=True), guard=f"pWb{ncc % 2}",
                                tick=True)
                            twb = em.dep("tensor")
                            em.emit("scalar",
                                    lambda e, ncc=ncc, c=c, half=half: e.activation(
                                        gcol[:, ncc % 2, :], pCc[ncc % 3][:],
                                        AF.Gelu,
                                        bias=bcT[:, half, c:c + 1]),
                                    after=[("tensor", tq)],
                                    guard=f"gcol{ncc % 2}", tick=True,
                                    selfw=False, set_guard=f"pCc{ncc % 3}")
                            em.emit("vector", lambda e, ncc=ncc: e.tensor_mul(
                                colsw[:, ncc, :], gcol[:, ncc % 2, :],
                                pWb[ncc % 2][:]),
                                after=[("scalar", em.dep("scalar")),
                                       ("tensor", twb)],
                                guard=("colsw" if ncc == 0 else None),
                                tick=True, selfw=False,
                                set_guard=[f"gcol{ncc % 2}", f"pWb{ncc % 2}"])
                            colsw_v[ncc] = em.dep("vector")
                            if ncc >= 2:
                                pj = ncc - 2
                                for i in range(2):
                                    em.emit("tensor",
                                            lambda e, i=i, pj=pj: e.matmul(
                                                pP[i][:],
                                                colsw[:, pj, 128 * i:128 * i + 128],
                                                pwb[:, pj, 0:512],
                                                start=(pj == 0), stop=False),
                                            after=[("vector", colsw_v[pj])],
                                            guard=(f"pP{i}" if pj == 0 else None))
                        for pj in (30, 31):
                            for i in range(2):
                                em.emit("tensor",
                                        lambda e, i=i, pj=pj: e.matmul(
                                            pP[i][:],
                                            colsw[:, pj, 128 * i:128 * i + 128],
                                            pwb[:, pj, 0:512],
                                            start=False, stop=(pj == 31)),
                                        after=[("vector", colsw_v[pj])],
                                        tick=(pj == 31))
                        cols_v = em.dep("vector")
                        em.guards["xn2f"] = ("tensor", em.dep("tensor"))
                        for i in range(2):
                            t = 2 * tb + i
                            em.emit("scalar", lambda e, i=i, s=s, t=t: e.activation(
                                ybuf[:, i, :], x2l[:, i, :], AF.Identity,
                                bias=statb[:, s, t, 2:3], scale=statb[:, s, t, 1:2]),
                                guard=f"ybuf{i}", tick=True,
                                set_guard=f"x2l{i}")
                        yb_s = em.dep("scalar")
                        for oh in range(2):
                            for i in range(2):
                                if oh == 1:
                                    for ncc in range(32):
                                        em.emit("tensor",
                                                lambda e, i=i, ncc=ncc: e.matmul(
                                                    pP[i][:],
                                                    colsw[:, ncc, 128 * i:128 * i + 128],
                                                    pwb[:, ncc, 512:1024],
                                                    start=(ncc == 0), stop=(ncc == 31)),
                                                after=[("vector", cols_v)],
                                                guard=(f"pP{i}" if ncc == 0 else None),
                                                tick=(ncc == 31),
                                                set_guard=("colsw" if (i == 1
                                                                       and ncc == 31)
                                                           else None))
                                em.emit("vector", lambda e, oh=oh, i=i: e.tensor_add(
                                    ybuf[:, i, 512 * oh:512 * oh + 512], pP[i][:],
                                    ybuf[:, i, 512 * oh:512 * oh + 512]),
                                    after=[("tensor", em.dep("tensor")),
                                           ("scalar", yb_s), ("mids", pbg)],
                                    tick=True)
                                em.emit("vector", lambda e, oh=oh, i=i: e.tensor_add(
                                    ybuf[:, i, 512 * oh:512 * oh + 512],
                                    ybuf[:, i, 512 * oh:512 * oh + 512],
                                    pb_bc[:, 512 * oh:512 * oh + 512]),
                                    tick=True, set_guard=f"pP{i}")
                        vq = em.dep("vector")
                        for i in range(2):
                            t = 2 * tb + i
                            em.dma("gpsimd", ysp_d[s, 128 * t:128 * t + 128, :],
                                   ybuf[:, i, :], stream=f"yo{i}",
                                   after=[("vector", vq)],
                                   set_guard=f"ybuf{i}")
                        if tb + 1 < NTILES // 2:
                            tr_v = b2_prelude(tb + 1)

            def gen_b3(s):
                em.phase = f"b3_{s}"
                with ExitStack() as b3_es:
                    yb = b3_es.enter_context(sb_t(f"yb@{s}", [128, 2, D], F32))
                    ynscr = b3_es.enter_context(sb_t(f"ynscr@{s}", [128, 512], F32))
                    bst = b3_es.enter_context(sb_t(f"bst@{s}", [128, 10], F32))
                    brst = b3_es.enter_context(sb_t(f"brst@{s}", [128, 2], F32))
                    bnb2 = b3_es.enter_context(sb_t(f"bnb2@{s}", [128, 2], F32))
                    ysp_tot = [("yo0", em.dep("yo0")), ("yo1", em.dep("yo1")),
                               ("tensor", em.dep("tensor")),
                               ("vector", em.dep("vector")),
                               ("scalar", em.dep("scalar"))]
                    for tp in range(NTILES // 2):
                        em.phase = f"b3_{s}"
                        ylds = []
                        x2lds = []
                        for i in range(2):
                            t = 2 * tp + i
                            ylds.append(em.dma(
                                "sync", yb[:, i, :],
                                ysp_d[s, 128 * t:128 * t + 128, :],
                                stream=f"yi{i}", after=ysp_tot,
                                guard=f"yb{i}"))
                            x2lds.append(em.dma(
                                "sync", x2l[:, i, :],
                                x2sp_d[s, 128 * t:128 * t + 128, :],
                                stream=f"x2m{i}", after=ysp_tot,
                                guard=f"x2l{i}"))
                        for i in range(2):
                            em.emit("vector", lambda e, i=i: e.scalar_tensor_tensor(
                                ynscr[:], yb[:, i, 0:512], 1.0, yb[:, i, 0:512],
                                ALU.mult, ALU.mult, accum_out=bst[:, i:i + 1]),
                                after=[(f"yi{i}", ylds[i])], guard="b3scr",
                                tick=True)
                            em.emit("vector", lambda e, i=i: e.scalar_tensor_tensor(
                                ynscr[:], yb[:, i, 512:1024], 1.0,
                                yb[:, i, 512:1024],
                                ALU.mult, ALU.mult,
                                accum_out=bst[:, 8 + i:9 + i]),
                                tick=True)
                            em.emit("vector", lambda e, i=i: e.reduce_sum(
                                bst[:, 2 + i:3 + i], yb[:, i, :], axis=AX),
                                tick=True, selfw=False)
                        em.emit("vector", lambda e: e.tensor_add(
                            bst[:, 0:2], bst[:, 0:2], bst[:, 8:10]), tick=True)
                        em.emit("vector", lambda e: e.tensor_scalar(
                            bst[:, 4:6], bst[:, 0:2], 1.0 / D, EPS, ALU.mult,
                            ALU.add), tick=True)
                        em.emit("vector", lambda e: e.tensor_scalar_mul(
                            bst[:, 2:4], bst[:, 2:4], -1.0 / D), tick=True)
                        em.emit("vector", lambda e: e.tensor_mul(
                            bst[:, 6:8], bst[:, 2:4], bst[:, 2:4]), tick=True)
                        em.emit("vector", lambda e: e.tensor_sub(
                            bst[:, 4:6], bst[:, 4:6], bst[:, 6:8]), tick=True)
                        vq = em.dep("vector")
                        em.emit("scalar", lambda e: e.activation(
                            bst[:, 6:8], bst[:, 4:6], AF.Ln),
                            after=[("vector", vq),
                                   ("scalar!", em.dep("scalar"))], tick=True)
                        em.emit("scalar", lambda e: e.activation(
                            brst[:], bst[:, 6:8], AF.Exp, scale=-0.5),
                            after=[("scalar!", em.dep("scalar"))], tick=True)
                        em.emit("vector", lambda e: e.tensor_mul(
                            bnb2[:], bst[:, 2:4], brst[:]),
                            after=[("scalar", em.dep("scalar"))], tick=True)
                        vq = em.dep("vector")
                        for i in range(2):
                            t = 2 * tp + i
                            for hf in range(2):
                                em.emit("scalar", lambda e, i=i, hf=hf: e.activation(
                                    ynscr[:], yb[:, i, 512 * hf:512 * hf + 512],
                                    AF.Identity,
                                    bias=bnb2[:, i:i + 1], scale=brst[:, i:i + 1]),
                                    after=[("vector", vq)], guard="ynscr",
                                    tick=True,
                                    set_guard=(["b3scr", f"yb{i}"]
                                               if i == 1 and hf == 1
                                               else (f"yb{i}" if hf == 1
                                                     else None)))
                                sq = em.dep("scalar")
                                em.emit("vector", lambda e, i=i, hf=hf: e.tensor_mul(
                                    ybuf[:, i, 512 * hf:512 * hf + 512], ynscr[:],
                                    g_bc[:, 512 * hf:512 * hf + 512]),
                                    after=[("scalar", sq), ("mids", mid_s)],
                                    guard=(f"ybuf{i}" if hf == 0 else None),
                                    tick=True, set_guard="ynscr")
                                em.emit("vector", lambda e, i=i, hf=hf: e.tensor_add(
                                    ybuf[:, i, 512 * hf:512 * hf + 512],
                                    ybuf[:, i, 512 * hf:512 * hf + 512],
                                    b_bc[:, 512 * hf:512 * hf + 512]),
                                    tick=True)
                            em.emit("vector", lambda e, i=i: e.tensor_add(
                                ybuf[:, i, :], ybuf[:, i, :],
                                x2l[:, i, :]),
                                after=[(f"x2m{i}", x2lds[i])], tick=True,
                                set_guard=f"x2l{i}")
                            vq2 = em.dep("vector")
                            em.dma("gpsimd", out_d[s, 128 * t:128 * t + 128, :],
                                   ybuf[:, i, :], stream=f"out{i}",
                                   after=[("vector", vq2)],
                                   set_guard=f"ybuf{i}")
                        yield

            for _ in gen_b1(0):
                pass
            b2_phase(0, em.dep("vector"))
            g3 = gen_b3(0)
            g1 = gen_b1(1)
            next(g3)
            g3_left = NTILES // 2 - 1
            for _ in range(NTILES // 2 + 1):
                next(g1)
                if g3_left > 0:
                    next(g3)
                    g3_left -= 1
            for _ in g1:
                pass
            for _ in g3:
                pass
            b2_phase(1, em.dep("vector"))
            for _ in gen_b3(1):
                pass

        em.wait_all("sync", ["out0", "out1"])

        with ExitStack() as run_es:
            names = list(em.n) + list(em.streams)
            sems = {nm: run_es.enter_context(nc.semaphore(nm)) for nm in names}
            run_es.enter_context(nc.allow_low_precision(
                reason="float32r operands are the intended precision"))
            block = run_es.enter_context(nc.Block())
            em.replay(block, sems, nc)

    return nc


def kernel(L0_attn_x, L0_attn_Wq, L0_attn_Wk, L0_attn_Wv, L0_attn_Wo,
           L0_ffn_col_emb, L0_ffn_gate_w1, L0_ffn_gate_b1, L0_ffn_gate_w2,
           L0_ffn_gate_b2, L0_ffn_W_cols, L0_ffn_b_cols, L0_ffn_proj_w,
           L0_ffn_proj_b, L0_ffn_ln_g, L0_ffn_ln_b):
    x = np.ascontiguousarray(np.asarray(L0_attn_x, dtype=np.float32))
    B, T, _ = x.shape
    n_cores = 8
    spc = B // n_cores
    nc = build_program(spc, T)
    w = {
        "wq": np.ascontiguousarray(L0_attn_Wq, np.float32),
        "wk": np.ascontiguousarray(L0_attn_Wk, np.float32),
        "wv": np.ascontiguousarray(L0_attn_Wv, np.float32),
        "wo": np.ascontiguousarray(L0_attn_Wo, np.float32),
        "ce": np.ascontiguousarray(L0_ffn_col_emb, np.float32),
        "gw1": np.ascontiguousarray(L0_ffn_gate_w1, np.float32),
        "gb1": np.ascontiguousarray(L0_ffn_gate_b1, np.float32),
        "gw2": np.ascontiguousarray(L0_ffn_gate_w2, np.float32),
        "gb2": np.ascontiguousarray(L0_ffn_gate_b2, np.float32),
        "wc": np.ascontiguousarray(L0_ffn_W_cols, np.float32),
        "bc": np.ascontiguousarray(L0_ffn_b_cols, np.float32),
        "pw": np.ascontiguousarray(L0_ffn_proj_w, np.float32),
        "pb": np.ascontiguousarray(L0_ffn_proj_b, np.float32).reshape(1, -1),
        "lng": np.ascontiguousarray(L0_ffn_ln_g, np.float32).reshape(1, -1),
        "lnb": np.ascontiguousarray(L0_ffn_ln_b, np.float32).reshape(1, -1),
    }
    in_maps = [dict(w, x=np.ascontiguousarray(x[c * spc:(c + 1) * spc]))
               for c in range(n_cores)]
    res = run_bass_kernel_spmd(nc, in_maps, core_ids=list(range(n_cores)))
    return np.concatenate([res.results[c]["out"] for c in range(n_cores)], axis=0)


# version 24
# speedup vs baseline: 1.0440x; 1.0440x over previous
import numpy as np
import ml_dtypes
import concourse.bass as bass
import concourse.mybir as mybir
from concourse.bass_utils import run_bass_kernel_spmd
from contextlib import ExitStack

F32 = mybir.dt.float32
F32R = mybir.dt.float32r
BF16 = mybir.dt.bfloat16
AF = mybir.ActivationFunctionType
ALU = mybir.AluOpType
AX = mybir.AxisListType.X

D = 1024
KC = 8
H = 16
HD = 64
C = 16
NN = 256
DFF = 4096
DH = 512
EPS = 1e-5


class Em:

    ENG = ("tensor", "vector", "scalar", "sync", "gpsimd")

    def __init__(self):
        self.phase = "init"
        self.items = {e: [] for e in self.ENG}
        self.n = {"tensor": 0, "vector": 0, "scalar": 0}
        self.streams = {}
        self.waited = {e: {} for e in self.ENG}
        self.guards = {}

    def dep(self, sem):
        return self.n[sem] if sem in self.n else self.streams.get(sem, 0)

    def emit(self, eng, fn, after=(), tick=False, guard=None, set_guard=None,
             selfw=True):
        assert eng in ("tensor", "vector", "scalar")
        after = list(after)
        if eng in ("vector", "scalar"):
            if selfw and self.n[eng]:
                after.append((eng + "!", self.n[eng]))
            tick = True
        if guard:
            for g in (guard if isinstance(guard, (list, tuple)) else [guard]):
                if g in self.guards:
                    after.append(self.guards[g])
        waits = []
        for p, v in after:
            force = p.endswith("!")
            p = p.rstrip("!")
            if p == eng and not force:
                continue
            have = self.n[p] if p in self.n else self.streams.get(p, 0)
            assert v <= have, f"forward wait {p}>={v} (have {have})"
            if force or self.waited[eng].get(p, 0) < v:
                waits.append((p, v))
                self.waited[eng][p] = v
        if tick:
            self.n[eng] += 1
        if set_guard:
            assert tick
            for g in (set_guard if isinstance(set_guard, (list, tuple))
                      else [set_guard]):
                self.guards[g] = (eng, self.n[eng])
        self.items[eng].append((fn, waits, tick, eng, self.phase))

    def dma(self, eng, out_ap, in_ap, stream, after=(), guard=None,
            set_guard=None):
        assert eng in ("sync", "gpsimd")
        after = list(after)
        if guard:
            for g in (guard if isinstance(guard, (list, tuple)) else [guard]):
                if g in self.guards:
                    after.append(self.guards[g])
        waits = []
        for p, v in after:
            have = self.n[p] if p in self.n else self.streams.get(p, 0)
            assert v <= have, f"forward wait {p}>={v} (have {have})"
            if self.waited[eng].get(p, 0) < v:
                waits.append((p, v))
                self.waited[eng][p] = v
        self.streams[stream] = self.streams.get(stream, 0) + 16
        if set_guard:
            for g in (set_guard if isinstance(set_guard, (list, tuple))
                      else [set_guard]):
                self.guards[g] = (stream, self.streams[stream])
        self.items[eng].append(
            (lambda e, o=out_ap, i=in_ap: e.dma_start(out=o, in_=i),
             waits, True, stream, self.phase))
        return self.streams[stream]

    def wait_all(self, eng, sems):
        for p in sems:
            v = self.dep(p)
            if v and self.waited[eng].get(p, 0) < v:
                self.items[eng].append(
                    (lambda e, p=p, v=v: None, [(p, v)], False, eng, self.phase))
                self.waited[eng][p] = v

    def replay(self, block, sems, nc=None):
        def runner(name):
            def run(eng_obj):
                cur = None
                for fn, waits, tick, target, phase in self.items[name]:
                    if nc is not None and phase != cur:
                        if cur is not None:
                            nc.leave_named_scope(cur, 0, False)
                        nc.enter_named_scope(phase, False)
                        cur = phase
                    for p, v in waits:
                        eng_obj.wait_ge(sems[p], v)
                    inst = fn(eng_obj)
                    if tick and inst is not None:
                        inst.then_inc(sems[target],
                                      16 if target not in self.n else 1)
                if nc is not None and cur is not None:
                    nc.leave_named_scope(cur, 0, False)
            return run
        block.tensor(runner("tensor"))
        block.vector(runner("vector"))
        block.scalar(runner("scalar"))
        block.sync(runner("sync"))
        block.gpsimd(runner("gpsimd"))


def build_program(SEQ, T, detect_races=False, dbg_spill=False):
    NSUP = T // 512
    NTILES = T // 128
    nc = bass.Bass("TRN2", target_bir_lowering=False,
                   detect_race_conditions=detect_races)

    x_d = nc.dram_tensor("x", [SEQ, T, D], F32, kind="ExternalInput").ap()
    wq_d = nc.dram_tensor("wq", [D, D], F32, kind="ExternalInput").ap()
    wk_d = nc.dram_tensor("wk", [D, D], F32, kind="ExternalInput").ap()
    wv_d = nc.dram_tensor("wv", [D, D], F32, kind="ExternalInput").ap()
    wo_d = nc.dram_tensor("wo", [D, D], F32, kind="ExternalInput").ap()
    ce_d = nc.dram_tensor("ce", [C, D], F32, kind="ExternalInput").ap()
    gw1_d = nc.dram_tensor("gw1", [D, DH], F32, kind="ExternalInput").ap()
    gb1_d = nc.dram_tensor("gb1", [DH], F32, kind="ExternalInput").ap()
    gw2_d = nc.dram_tensor("gw2", [DH, C], F32, kind="ExternalInput").ap()
    gb2_d = nc.dram_tensor("gb2", [C], F32, kind="ExternalInput").ap()
    wc_d = nc.dram_tensor("wc", [C, D, NN], F32, kind="ExternalInput").ap()
    bc_d = nc.dram_tensor("bc", [C, NN], F32, kind="ExternalInput").ap()
    pw_d = nc.dram_tensor("pw", [DFF, D], F32, kind="ExternalInput").ap()
    pb_d = nc.dram_tensor("pb", [1, D], F32, kind="ExternalInput").ap()
    lng_d = nc.dram_tensor("lng", [1, D], F32, kind="ExternalInput").ap()
    lnb_d = nc.dram_tensor("lnb", [1, D], F32, kind="ExternalInput").ap()
    out_d = nc.dram_tensor("out", [SEQ, T, D], F32, kind="ExternalOutput").ap()
    spill_kind = "ExternalOutput" if dbg_spill else "Internal"
    x2sp_d = nc.dram_tensor("x2sp", [SEQ, T, D], F32, kind=spill_kind).ap()
    ysp_d = nc.dram_tensor("ysp", [SEQ, T, D], F32, kind=spill_kind).ap()
    nrm_d = nc.dram_tensor("nrm_d", [16, 512], F32, kind="Internal").ap()
    idb_d = nc.inline_tensor(np.eye(128, dtype=ml_dtypes.bfloat16), "idb_c").ap()
    idf_d = nc.inline_tensor(np.eye(128, dtype=np.float32), "idf_c").ap()
    ones_d = nc.inline_tensor(np.ones((1, 128), dtype=np.float32), "ones_c").ap()
    sel_np = np.zeros((C, C * 128), dtype=np.float32)
    for _c in range(C):
        sel_np[_c, _c * 128:(_c + 1) * 128] = 1.0
    sel_d = nc.inline_tensor(sel_np, "sel_c").ap()

    _names = {}
    _reuse = {}

    def _uniq(name):
        k = _names.get(name, 0)
        _names[name] = k + 1
        return name if k == 0 else f"{name}_{k}"

    class _ReuseGuard:

        def __init__(self, inner, key, space):
            self.inner = inner
            self.key = key
            self.space = space

        def __enter__(self):
            addr = nc.bytes_allocated(self.space)
            h = self.inner.__enter__()
            if self.key in _reuse:
                old_addr, old_h = _reuse[self.key]
                assert old_addr == addr, f"layout drift for {self.key}"
                return old_h
            _reuse[self.key] = (addr, h)
            return h

        def __exit__(self, *a):
            return self.inner.__exit__(*a)

    import concourse.bass_types as _bt

    def sb_t(name, shape, dt=F32):
        return _ReuseGuard(nc.sbuf_tensor(_uniq(name), shape, dt),
                           ("sb", name, tuple(shape), dt), bass.MemorySpace.SBUF)

    def ps_t(name, shape, dt=F32):
        return _ReuseGuard(nc.psum_tensor(_uniq(name), shape, dt),
                           ("ps", name, tuple(shape), dt), bass.MemorySpace.PSUM)

    em = Em()
    es = ExitStack()

    with es:
        def sbuf(name, shape, dt=F32):
            return es.enter_context(sb_t(name, shape, dt))

        idb = sbuf("idb", [128, 128], BF16)
        idf = sbuf("idf", [128, 128], F32)
        ones_r = sbuf("ones_r", [1, 128], F32R)
        b1T = sbuf("b1T", [128, 4], F32)
        b2r = sbuf("b2r", [1, C], F32R)
        bcT = sbuf("bcT", [128, 2, C], F32)
        ceT_r = sbuf("ceT_r", [128, KC, C], F32R)
        statb = sbuf("statb", [128, SEQ, NTILES, 4], F32)
        W1r = sbuf("W1r", [128, KC, DH], F32R)
        W2r = sbuf("W2r", [128, 4, C], F32R)


        with ExitStack() as st_es:
            stage = st_es.enter_context(sb_t("stage", [128, DH], F32))
            ones1 = st_es.enter_context(sb_t("ones1", [1, 128], F32))
            stage2 = st_es.enter_context(sb_t("stage2", [16, 1024], F32))
            sc_st = st_es.enter_context(sb_t("sc_st", [16, 1024], F32))
            ssq16 = st_es.enter_context(sb_t("ssq16", [16, 1], F32))
            l16 = st_es.enter_context(sb_t("l16", [16, 1], F32))
            b1s = st_es.enter_context(sb_t("b1s", [4, 128], F32))
            b2s = st_es.enter_context(sb_t("b2s", [1, C], F32))
            bcs = st_es.enter_context(sb_t("bcs", [16, 256], F32))
            em.dma("sync", idb[:], idb_d, stream="ini")
            em.dma("sync", idf[:], idf_d, stream="ini")
            em.dma("sync", ones1[:], ones_d, stream="ini")
            em.dma("sync", b1s[:], gb1_d.rearrange("(c p) -> c p", p=128),
                   stream="ini")
            em.dma("sync", b2s[:], gb2_d.unsqueeze(0), stream="ini")
            em.dma("sync", bcs[:], bc_d, stream="ini")
            em.dma("sync", stage2[:], ce_d, stream="ini")
            ini = em.dep("ini")
            em.emit("vector", lambda e: e.tensor_copy(ones_r[:], ones1[:]),
                    after=[("ini", ini)], tick=True)
            pTi = st_es.enter_context(ps_t("pTi", [128, 128], F32))

            em.emit("tensor", lambda e: e.transpose(pTi[:, 0:4], b1s[:], idf[0:4, 0:4]),
                    after=[("ini", ini)], tick=True)
            em.emit("vector", lambda e: e.tensor_copy(b1T[:], pTi[:, 0:4]),
                    after=[("tensor", em.dep("tensor"))], tick=True)
            em.emit("vector", lambda e: e.tensor_copy(b2r[:], b2s[:]),
                    after=[("ini", ini)], tick=True)
            for half in range(2):
                em.emit("tensor",
                        lambda e, h=half: e.transpose(
                            pTi[:, 32 + 16 * h:48 + 16 * h],
                            bcs[:, 128 * h:128 * h + 128], idf[0:16, 0:16]),
                        after=[("ini", ini), ("vector", em.dep("vector"))], tick=True)
                em.emit("vector",
                        lambda e, h=half: e.tensor_copy(
                            bcT[:, h, :], pTi[:, 32 + 16 * h:48 + 16 * h]),
                        after=[("tensor", em.dep("tensor"))], tick=True)
            em.emit("vector", lambda e: e.scalar_tensor_tensor(
                sc_st[:], stage2[:], 1.0, stage2[:], ALU.mult, ALU.mult,
                accum_out=ssq16[:]), after=[("ini", ini)], tick=True)
            em.emit("vector", lambda e: e.tensor_scalar_max(ssq16[:], ssq16[:], 1e-30),
                    after=[("vector!", em.dep("vector"))], tick=True)
            em.emit("scalar", lambda e: e.activation(l16[:], ssq16[:], AF.Ln),
                    after=[("vector", em.dep("vector"))], tick=True)
            em.emit("scalar", lambda e: e.activation(l16[:], l16[:], AF.Exp, scale=-0.5),
                    after=[("scalar!", em.dep("scalar"))], tick=True)
            em.emit("vector", lambda e: e.tensor_scalar_mul(stage2[:], stage2[:], l16[:]),
                    after=[("scalar", em.dep("scalar"))], tick=True)
            vq = em.dep("vector")
            for kc in range(KC):
                em.emit("tensor", lambda e, k=kc: e.transpose(
                    pTi[:, 64:80], stage2[:, 128 * k:128 * k + 128], idf[0:16, 0:16]),
                    after=[("vector", vq)], guard="pTi2", tick=True)
                em.emit("vector", lambda e, k=kc: e.tensor_copy(ceT_r[:, k, :],
                                                                pTi[:, 64:80]),
                        after=[("tensor", em.dep("tensor"))], tick=True,
                        set_guard="pTi2")
            for kc in range(KC):
                v = em.dma("sync", stage[:], gw1_d[128 * kc:128 * kc + 128, :],
                           stream="stg", guard="stage")
                em.emit("vector", lambda e, k=kc: e.tensor_copy(W1r[:, k, :], stage[:]),
                        after=[("stg", v)], tick=True, set_guard="stage")
            for kc in range(4):
                v = em.dma("sync", stage[:, 0:C], gw2_d[128 * kc:128 * kc + 128, :],
                           stream="stg", guard="stage")
                em.emit("vector", lambda e, k=kc: e.tensor_copy(W2r[:, k, :],
                                                                stage[:, 0:C]),
                        after=[("stg", v)], tick=True, set_guard="stage")
            init_v = em.dep("vector")
            init_t = em.dep("tensor")
            init_s = em.dep("scalar")

        with ExitStack() as a_es:
            def asb(name, shape, dt=F32):
                return a_es.enter_context(sb_t(name, shape, dt))
            Wq_sb = asb("Wq_sb", [128, KC, D], BF16)
            Wk_sb = asb("Wk_sb", [128, KC, D], BF16)
            Wv_sb = asb("Wv_sb", [128, KC, D], BF16)
            Wo_sb = asb("Wo_sb", [128, KC, D], BF16)
            k_f = asb("k_f", [128, KC, T], BF16)
            v_on = asb("v_on", [128, NTILES, H, 65], BF16)
            x1b = asb("x1b", [128, 4, D], F32)
            q_f = asb("q_f", [128, KC, 512], BF16)
            att_f = asb("att_f", [128, KC, 512], BF16)
            x2b = asb("x2b", [128, 2, D], F32)
            sqscr = asb("sqscr", [128, 512], F32)
            ssqN2 = asb("ssqN2", [128, 16], F32)
            ssqN = asb("ssqN", [128, 16], F32)
            smN = asb("smN", [128, 16], F32)
            m2N = asb("m2N", [128, 16], F32)
            tvN = asb("tvN", [128, 16], F32)
            rvN = asb("rvN", [128, 16], F32)
            vaN = asb("vaN", [128, 16], F32)
            ltN = asb("ltN", [128, 16], F32)

            for w_sb, w_d in ((Wq_sb, wq_d), (Wk_sb, wk_d), (Wv_sb, wv_d),
                              (Wo_sb, wo_d)):
                em.dma("gpsimd", w_sb[:], w_d.rearrange("(kc p) o -> p kc o", p=128),
                       stream="wld", after=[("vector", init_v), ("tensor", init_t)])
            wload = em.dep("wld")

            def ln_big(x_ap, col, after):
                em.emit("vector", lambda e: e.scalar_tensor_tensor(
                    sqscr[:], x_ap[:, 0:512], 1.0, x_ap[:, 0:512], ALU.mult,
                    ALU.mult, accum_out=ssqN[:, col:col + 1]),
                    after=after, tick=True)
                em.emit("vector", lambda e: e.scalar_tensor_tensor(
                    sqscr[:], x_ap[:, 512:1024], 1.0, x_ap[:, 512:1024], ALU.mult,
                    ALU.mult, accum_out=ssqN2[:, col:col + 1]), tick=True)
                em.emit("vector", lambda e: e.reduce_sum(
                    smN[:, col:col + 1], x_ap, axis=AX), tick=True)
                return em.dep("vector")

            def ln_tail(s, t0, nt, want_invn):
                negm = statb[:, s, t0:t0 + nt, 0]
                rstd = statb[:, s, t0:t0 + nt, 1]
                nb = statb[:, s, t0:t0 + nt, 2]
                invn = statb[:, s, t0:t0 + nt, 3]
                em.emit("vector", lambda e: e.tensor_add(
                    ssqN[:, 0:nt], ssqN[:, 0:nt], ssqN2[:, 0:nt]),
                    guard="lnscr", tick=True)
                em.emit("vector", lambda e: e.tensor_scalar(
                    tvN[:, 0:nt], ssqN[:, 0:nt], 1.0 / D, EPS, ALU.mult, ALU.add),
                    tick=True)
                em.emit("vector", lambda e: e.tensor_scalar_mul(
                    negm, smN[:, 0:nt], -1.0 / D), tick=True)
                em.emit("vector", lambda e: e.tensor_mul(m2N[:, 0:nt], negm, negm),
                        tick=True)
                em.emit("vector", lambda e: e.tensor_sub(
                    tvN[:, 0:nt], tvN[:, 0:nt], m2N[:, 0:nt]), tick=True)
                vq = em.dep("vector")
                em.emit("scalar", lambda e: e.activation(ltN[:, 0:nt], tvN[:, 0:nt],
                                                         AF.Ln),
                        after=[("vector", vq), ("scalar!", em.dep("scalar"))],
                        tick=True)
                em.emit("scalar", lambda e: e.activation(rstd, ltN[:, 0:nt], AF.Exp,
                                                         scale=-0.5),
                        after=[("scalar!", em.dep("scalar"))], tick=True,
                        set_guard=(None if want_invn else "lnscr"))
                em.emit("vector", lambda e: e.tensor_mul(nb, negm, rstd),
                        after=[("scalar", em.dep("scalar"))], tick=True)
                if want_invn:
                    em.emit("vector", lambda e: e.reciprocal(rvN[:, 0:nt],
                                                             tvN[:, 0:nt]), tick=True)
                    em.emit("vector", lambda e: e.tensor_scalar_add(
                        vaN[:, 0:nt], tvN[:, 0:nt], -EPS), tick=True)
                    em.emit("vector", lambda e: e.tensor_mul(
                        vaN[:, 0:nt], vaN[:, 0:nt], rvN[:, 0:nt]), tick=True)
                    em.emit("vector", lambda e: e.tensor_scalar(
                        vaN[:, 0:nt], vaN[:, 0:nt], float(D), 1e-24, ALU.mult,
                        ALU.max), tick=True)
                    vq = em.dep("vector")
                    em.emit("scalar", lambda e: e.activation(ltN[:, 0:nt],
                                                             vaN[:, 0:nt], AF.Ln),
                            after=[("vector", vq),
                                   ("scalar!", em.dep("scalar"))], tick=True)
                    em.emit("scalar", lambda e: e.activation(invn, ltN[:, 0:nt],
                                                             AF.Exp, scale=-0.5),
                            after=[("scalar!", em.dep("scalar"))], tick=True,
                            set_guard="lnscr")
                return em.dep("vector"), em.dep("scalar")

            for s in range(SEQ):
                em.phase = f"A{s}"
                em.emit("vector", lambda e: e.memset(v_on[:, :, :, 64:65], 1.0),
                        after=[("tensor", em.dep("tensor"))], tick=True)
                vones = em.dep("vector")

                for u in range(NSUP):
                    with ExitStack() as ps_es:
                        xn1f = ps_es.enter_context(
                            sb_t("xn1f", [128, KC, 512], BF16))
                        xn1tm = ps_es.enter_context(
                            sb_t("xn1tm", [128, 2, D], BF16))
                        vtmp = ps_es.enter_context(sb_t("vtmp", [128, 512], BF16))
                        pA = [ps_es.enter_context(
                            ps_t(f"pA{i}", [128, 512], F32)) for i in range(3)]
                        pTb = [ps_es.enter_context(
                            ps_t(f"pTb{i}", [128, 128], BF16)) for i in range(4)]

                        em.phase = f"qkv{s}_{u}"
                        qev_kc = {}
                        xvs = []
                        for i in range(4):
                            xvs.append(em.dma(
                                "sync", x1b[:, i, :],
                                x_d[s, 512 * u + 128 * i:512 * u + 128 * i + 128, :],
                                stream=f"x1{i}", guard=f"x1b{i}",
                                after=[("scalar", em.dep("scalar"))]))
                        for i in range(4):
                            ln_big(x1b[:, i, :], i, [(f"x1{i}", xvs[i])])
                        vq, sq0 = ln_tail(s, 4 * u, 4, True)
                        for i in range(4):
                            t = 4 * u + i
                            em.emit("scalar", lambda e, i=i, s=s, t=t: e.activation(
                                xn1tm[:, i % 2, :], x1b[:, i, :], AF.Identity,
                                bias=statb[:, s, t, 2:3], scale=statb[:, s, t, 1:2]),
                                after=[("vector", vq), ("scalar", sq0)],
                                guard=f"xn1tm{i % 2}",
                                tick=True)
                            sq = em.dep("scalar")
                            for kc in range(KC):
                                em.emit("tensor", lambda e, i=i, kc=kc: e.transpose(
                                    pTb[kc % 4][:],
                                    xn1tm[:, i % 2, 128 * kc:128 * kc + 128], idb[:]),
                                    after=[("scalar", sq), ("ini", 112)],
                                    guard=f"pTb{kc % 4}", tick=True,
                                    set_guard=(f"xn1tm{i % 2}" if kc == KC - 1 else None))
                                em.emit("vector", lambda e, i=i, kc=kc: e.tensor_copy(
                                    xn1f[:, kc, 128 * i:128 * i + 128], pTb[kc % 4][:]),
                                    after=[("tensor", em.dep("tensor"))], tick=True,
                                    selfw=False, set_guard=f"pTb{kc % 4}")
                                qev_kc[kc] = em.dep("vector")
                        xn1_done = em.dep("vector")

                        for oc in range(KC):
                            for wi, wsb in enumerate((Wq_sb, Wk_sb, Wv_sb)):
                                pb = pA[wi]
                                for kc in range(KC):
                                    em.emit("tensor",
                                            lambda e, wsb=wsb, kc=kc, oc=oc, pb=pb: e.matmul(
                                                pb[:],
                                                wsb[:, kc, 128 * oc:128 * oc + 128],
                                                xn1f[:, kc, :], start=(kc == 0),
                                                stop=(kc == KC - 1)),
                                            after=[("vector", xn1_done),
                                                   ("wld", wload)],
                                            guard=(f"pA{wi}" if kc == 0 else None),
                                            tick=(kc == KC - 1))
                                tq = em.dep("tensor")
                                if wi == 0:
                                    em.emit("vector", lambda e, oc=oc, pb=pb: e.tensor_copy(
                                        q_f[:, oc, :], pb[:]), after=[("tensor", tq)],
                                        tick=True, selfw=False, set_guard=f"pA{wi}")
                                elif wi == 1:
                                    em.emit("vector",
                                            lambda e, oc=oc, u=u, pb=pb: e.tensor_copy(
                                                k_f[:, oc, 512 * u:512 * u + 512], pb[:]),
                                            after=[("tensor", tq)], tick=True,
                                            selfw=False, set_guard=f"pA{wi}")
                                else:
                                    em.emit("vector", lambda e, pb=pb: e.tensor_copy(
                                        vtmp[:], pb[:]), after=[("tensor", tq)],
                                        tick=True, set_guard=f"pA{wi}")
                                    vq = em.dep("vector")
                                    for j in range(4):
                                        em.emit("tensor", lambda e, j=j: e.transpose(
                                            pTb[j][:], vtmp[:, 128 * j:128 * j + 128],
                                            idb[:]),
                                            after=[("vector", vq)],
                                            guard=f"pTb{j}", tick=True)
                                        dstv = v_on[:, 4 * u + j, 2 * oc:2 * oc + 2, 0:64]
                                        em.emit("vector",
                                                lambda e, j=j, dstv=dstv: e.tensor_copy(
                                                    dstv,
                                                    pTb[j][:].rearrange(
                                                        "p (h c) -> p h c", h=2)),
                                                after=[("tensor", em.dep("tensor")),
                                                       ("vector", vones)],
                                                tick=True, selfw=False,
                                                set_guard=f"pTb{j}")
                        qkv_v = em.dep("vector")

                    with ExitStack() as aw_es:
                        em.phase = f"attn{s}_{u}"
                        nrm = aw_es.enter_context(sb_t("nrm", [16, 512], F32))
                        rbw = aw_es.enter_context(sb_t("rbw", [128, 2, 512], F32))
                        scr = aw_es.enter_context(sb_t("scr", [1, 512], F32))
                        tt_ticks = {}

                        def norm_chain():
                            em.emit("vector", lambda e: e.reciprocal(
                                nrm[:], nrm[:]),
                                after=[("nrb0", em.dep("nrb0")),
                                       ("nrb1", em.dep("nrb1"))], tick=True)
                            nd = em.dma("gpsimd", nrm_d[:], nrm[:],
                                        stream="nrb0",
                                        after=[("vector", em.dep("vector"))])
                            for kc in range(KC):
                                gds = []
                                for i2 in range(2):
                                    st = f"nrb{2 + 2 * (kc % 2) + i2}"
                                    gds.append((st, em.dma(
                                        "gpsimd" if i2 == 0 else "sync",
                                        rbw[64 * i2:64 * i2 + 64, kc % 2, :],
                                        nrm_d[2 * kc + i2, None, :].broadcast_to(
                                            (64, 512)),
                                        stream=st,
                                        guard=f"rbw{kc % 2}",
                                        after=[("nrb0", nd)])))
                                em.emit("vector", lambda e, kc=kc: e.tensor_tensor(
                                    att_f[:, kc, :], att_f[:, kc, :],
                                    rbw[:, kc % 2, :],
                                    ALU.mult),
                                    after=gds, tick=True,
                                    set_guard=f"rbw{kc % 2}")
                                tt_ticks[kc] = em.dep("vector")

                        with ExitStack() as ps_es:
                            PT = ps_es.enter_context(
                                sb_t("PT", [128, 3, 1536], BF16))
                            pSc = [ps_es.enter_context(
                                ps_t(f"pSc{i}", [128, 1536], F32))
                                for i in range(2)]
                            pU = [ps_es.enter_context(
                                ps_t(f"pU{i}", [65, 512], F32)) for i in range(2)]
                            items = [(j, 0) for j in range(4 * u)]
                            items += [(4 * u + jl, 128 * jl)
                                      for jl in (0, 1, 3, 2)]
                            slots = []
                            cur, col = [], 0
                            for j, off in items:
                                n = 512 - off
                                c = col
                                if c // 512 != (c + n - 1) // 512:
                                    c = (c // 512 + 1) * 512
                                if c + n > 1536:
                                    slots.append((cur, col))
                                    cur, c = [], 0
                                cur.append((j, off, c))
                                col = c + n
                            slots.append((cur, col))
                            ng = len(slots)

                            def make_tail(h, pu):
                                oc, r0 = h // 2, (h % 2) * 64
                                av_t = em.dep("tensor")

                                def tail():
                                    em.emit("vector", lambda e: e.tensor_copy(
                                        scr[:], pu[64:65, :]),
                                        after=[("tensor", av_t)],
                                        guard="scr0", tick=True)
                                    em.dma("gpsimd", nrm[h:h + 1, :],
                                           scr[:],
                                           stream=f"nrb{h % 2}",
                                           after=[("vector", em.dep("vector"))],
                                           set_guard="scr0")
                                    em.emit("vector", lambda e: e.tensor_copy(
                                        att_f[r0:r0 + 64, oc, :], pu[0:64, :]),
                                        tick=True, set_guard=f"pU{h % 2}")
                                return tail

                            exp_ticks = {}
                            tails = {}
                            flat = [(h, gi) for h in range(H) for gi in range(ng)]
                            nfl = len(flat)
                            LAG = 2
                            for t_ in range(nfl + LAG):
                                if t_ < nfl:
                                    h, gi = flat[t_]
                                    oc, r0 = h // 2, (h % 2) * 64
                                    grp, wid = slots[gi]
                                    for j, off, col in grp:
                                        n = 512 - off
                                        em.emit("tensor",
                                                lambda e, j=j, off=off, n=n, col=col, oc=oc, r0=r0, t_=t_: e.matmul(
                                                    pSc[t_ % 2][:, col:col + n],
                                                    k_f[r0:r0 + 64, oc, 128 * j:128 * j + 128],
                                                    q_f[r0:r0 + 64, oc, off:512],
                                                    start=True, stop=True),
                                                after=[("vector", qkv_v)],
                                                guard=(f"pSc{t_ % 2}" if col == 0
                                                       else None), tick=True)
                                    em.emit("scalar", lambda e, wid=wid, t_=t_: e.activation(
                                        PT[:, t_ % 3, 0:wid], pSc[t_ % 2][:, 0:wid],
                                        AF.Exp, scale=0.125),
                                        after=[("tensor", em.dep("tensor"))],
                                        tick=True, selfw=False,
                                        set_guard=f"pSc{t_ % 2}")
                                    exp_ticks[t_] = em.dep("scalar")
                                if t_ - LAG in tails:
                                    tails.pop(t_ - LAG)[1]()
                                if t_ >= LAG:
                                    ai = t_ - LAG
                                    h, gi = flat[ai]
                                    pu = pU[h % 2]
                                    grp, wid = slots[gi]
                                    for mi, (j, off, col) in enumerate(grp):
                                        n = 512 - off
                                        first = (gi == 0 and mi == 0)
                                        last = (gi == ng - 1 and mi == len(grp) - 1)
                                        em.emit("tensor",
                                                lambda e, j=j, off=off, n=n, col=col, h=h, ai=ai, pu=pu, first=first, last=last: e.matmul(
                                                    pu[:, off:512], v_on[:, j, h, :],
                                                    PT[:, ai % 3, col:col + n],
                                                    start=first, stop=last),
                                                after=[("scalar", exp_ticks[ai])],
                                                guard=(f"pU{h % 2}" if first
                                                       else None),
                                                tick=last)
                                    if gi == ng - 1:
                                        tails[t_] = (h, make_tail(h, pu))
                            for k_ in sorted(tails):
                                tails.pop(k_)[1]()
                            norm_chain()

                        with ExitStack() as ps_es:
                            em.phase = f"wo{s}_{u}"
                            pW = [ps_es.enter_context(
                                ps_t(f"pW{i}", [128, 512], F32)) for i in range(8)]
                            mm_ticks = {}
                            for kc in range(KC):
                                for i in range(4):
                                    for oh in range(2):
                                        pb = pW[2 * i + oh]
                                        em.emit("tensor",
                                                lambda e, i=i, oh=oh, kc=kc, pb=pb: e.matmul(
                                                    pb[:],
                                                    att_f[:, kc, 128 * i:128 * i + 128],
                                                    Wo_sb[:, kc, 512 * oh:512 * oh + 512],
                                                    start=(kc == 0), stop=(kc == KC - 1)),
                                                after=[("vector", tt_ticks[kc])],
                                                guard=(f"pW{2 * i + oh}" if kc == 0
                                                       else None),
                                                tick=(kc == KC - 1))
                                        if kc == KC - 1:
                                            mm_ticks[2 * i + oh] = em.dep("tensor")
                            for i in range(4):
                                t = 4 * u + i
                                for oh in range(2):
                                    pb = pW[2 * i + oh]
                                    em.emit("vector",
                                            lambda e, i=i, oh=oh, pb=pb: e.tensor_add(
                                                x2b[:, i % 2, 512 * oh:512 * oh + 512],
                                                pb[:],
                                                x1b[:, i, 512 * oh:512 * oh + 512]),
                                            after=[("tensor", mm_ticks[2 * i + oh])],
                                            guard=(f"x2b{i % 2}" if oh == 0 else None),
                                            tick=True,
                                            set_guard=([f"pW{2 * i + oh}"]
                                                       + ([f"x1b{i}"] if oh == 1
                                                          else [])))
                                vq = em.dep("vector")
                                em.dma("gpsimd", x2sp_d[s, 128 * t:128 * t + 128, :],
                                       x2b[:, i % 2, :], stream=f"x2o{i % 2}",
                                       after=[("vector", vq)],
                                       set_guard=f"x2b{i % 2}")
                                ln_big(x2b[:, i % 2, :], i, [("vector", vq)])
                            ln_tail(s, 4 * u, 4, True)
            a_t, a_v, a_s = em.dep("tensor"), em.dep("vector"), em.dep("scalar")
            a_sp = [("x2o0", em.dep("x2o0")), ("x2o1", em.dep("x2o1"))]

        with ExitStack() as b_es:
            def bsb(name, shape, dt=F32):
                return b_es.enter_context(sb_t(name, shape, dt))
            Wcb = bsb("Wcb", [128, KC, DFF], BF16)
            pwb = bsb("pwb", [128, 32, D], BF16)
            g_bc = bsb("g_bc", [128, D], F32)
            b_bc = bsb("b_bc", [128, D], F32)
            x2l = bsb("x2l", [128, 2, D], F32)
            wbuf = bsb("wbuf", [128, NTILES, C], F32)
            ybuf = bsb("ybuf", [128, 2, D], F32)
            bsm = bsb("bsm", [128, 1], F32)
            btv = bsb("btv", [128, 1], F32)
            bm2 = bsb("bm2", [128, 1], F32)
            bltv = bsb("bltv", [128, 1], F32)
            brstd = bsb("brstd", [128, 1], F32)
            bnb = bsb("bnb", [128, 1], F32)

            em.phase = "mid"
            mid_after = [("tensor", a_t), ("vector", a_v), ("scalar", a_s),
                         ("x2o0", em.dep("x2o0")), ("x2o1", em.dep("x2o1")),
                         ("x10", em.dep("x10")), ("x11", em.dep("x11")),
                         ("x12", em.dep("x12")), ("x13", em.dep("x13")),
                         ("wld", em.dep("wld")),
                         ("nrb0", em.dep("nrb0")), ("nrb1", em.dep("nrb1")),
                         ("nrb2", em.dep("nrb2")), ("nrb3", em.dep("nrb3")),
                         ("nrb4", em.dep("nrb4")), ("nrb5", em.dep("nrb5"))]
            for kc in range(KC):
                em.dma("gpsimd", Wcb[:, kc, :].rearrange("p (c n) -> p c n", c=C),
                       wc_d.rearrange("c (kc p) n -> p kc c n", p=128)[:, kc, :, :],
                       stream="midg", after=mid_after)
            for g in range(4):
                em.dma("gpsimd", pwb[:, 8 * g:8 * g + 8, :],
                       pw_d.rearrange("(nc p) o -> p nc o", p=128)[:, 8 * g:8 * g + 8, :],
                       stream="midg", after=mid_after)

            em.dma("sync", g_bc[:], lng_d.broadcast_to((128, D)), stream="mids",
                   after=mid_after)
            em.dma("sync", b_bc[:], lnb_d.broadcast_to((128, D)), stream="mids",
                   after=mid_after)
            mid_g = em.dep("midg")
            mid_s = em.dep("mids")

            def gen_b1(s):
                em.phase = f"b1_{s}"
                with ExitStack() as b1_es:
                    h2buf = b1_es.enter_context(sb_t(f"h2buf@{s}", [128, NTILES, C], F32))
                    simbuf = b1_es.enter_context(sb_t(f"simbuf@{s}", [128, NTILES, C], F32))
                    gbuf = b1_es.enter_context(sb_t(f"gbuf@{s}", [128, NTILES, C], F32))
                    lgtb = b1_es.enter_context(
                        sb_t(f"lgtb@{s}", [128, NTILES, C], F32))
                    emxb = b1_es.enter_context(
                        sb_t(f"emxb@{s}", [128, NTILES], F32))
                    esumb = b1_es.enter_context(
                        sb_t(f"esumb@{s}", [128, NTILES], F32))
                    erinb = b1_es.enter_context(
                        sb_t(f"erinb@{s}", [128, NTILES], F32))
                    xn2tm = b1_es.enter_context(
                        sb_t(f"xn2tm@{s}", [128, 2, D], F32))
                    xnr = b1_es.enter_context(
                        sb_t(f"xnr@{s}", [128, KC, 256], F32R))
                    h1scr = b1_es.enter_context(
                        sb_t(f"h1scr@{s}", [128, 256], F32))
                    h1r = b1_es.enter_context(
                        sb_t(f"h1r@{s}", [128, 4, 256], F32R))
                    pG = [b1_es.enter_context(
                        ps_t(f"pG{i}@{s}", [128, 256], F32)) for i in range(4)]
                    pSG = b1_es.enter_context(ps_t(f"pSG@{s}", [128, 64], F32))
                    pTf = [b1_es.enter_context(
                        ps_t(f"pTf{i}@{s}", [128, 128], F32)) for i in range(3)]
                    b1sv, b1ss = em.dep("vector"), em.dep("scalar")
                    for u in range(NTILES // 2):
                        em.phase = f"b1_{s}"
                        for i in range(2):
                            t = 2 * u + i
                            ld = em.dma(
                                "sync", x2l[:, i, :],
                                x2sp_d[s, 128 * t:128 * t + 128, :],
                                stream=f"x2l{i}",
                                after=a_sp + [("vector", b1sv),
                                              ("scalar", b1ss)],
                                guard=f"x2l{i}")
                            em.emit("scalar", lambda e, i=i, s=s, t=t: e.activation(
                                xn2tm[:, i, :], x2l[:, i, :], AF.Identity,
                                bias=statb[:, s, t, 2:3], scale=statb[:, s, t, 1:2]),
                                after=[(f"x2l{i}", ld), ("vector", a_v)],
                                guard=f"xn2tm{i}", tick=True,
                                set_guard=f"x2l{i}")
                        apply_s = em.dep("scalar")
                        ev_kc = {}
                        for kc in range(KC):
                            for i in range(2):
                                em.emit("tensor", lambda e, kc=kc, i=i: e.transpose(
                                    pTf[(2 * kc + i) % 3][:],
                                    xn2tm[:, i, 128 * kc:128 * kc + 128], idf[:]),
                                    after=[("scalar", apply_s)],
                                    guard=f"pTf{(2 * kc + i) % 3}", tick=True,
                                    set_guard=(f"xn2tm{i}" if kc == KC - 1 else None))
                                em.emit("vector", lambda e, kc=kc, i=i: e.tensor_copy(
                                    xnr[:, kc, 128 * i:128 * i + 128],
                                    pTf[(2 * kc + i) % 3][:]),
                                    after=[("tensor", em.dep("tensor"))],
                                    guard=("xnr" if (kc == 0 and i == 0) else None),
                                    tick=True, selfw=False,
                                    set_guard=f"pTf{(2 * kc + i) % 3}")
                            ev_kc[kc] = em.dep("vector")
                        for kc in range(KC):
                            for oc in range(4):
                                em.emit("tensor", lambda e, kc=kc, oc=oc: e.matmul(
                                    pG[oc][:], W1r[:, kc, 128 * oc:128 * oc + 128],
                                    xnr[:, kc, :], start=(kc == 0),
                                    stop=(kc == KC - 1)),
                                    after=[("vector", ev_kc[kc])],
                                    guard=(f"pG{oc}" if kc == 0 else None),
                                    tick=(kc == KC - 1))
                        for kc in range(KC):
                            for i in range(2):
                                em.emit("tensor", lambda e, kc=kc, i=i: e.matmul(
                                    pSG[:, 16 * i:16 * i + 16],
                                    xnr[:, kc, 128 * i:128 * i + 128],
                                    ceT_r[:, kc, :],
                                    start=(kc == 0 and i == 0),
                                    stop=(kc == KC - 1 and i == 1)),
                                    after=[("vector", ev_kc[kc])],
                                    guard=("pSG" if (kc == 0 and i == 0) else None),
                                    tick=(kc == KC - 1 and i == 1),
                                    set_guard=("xnr" if (kc == KC - 1 and i == 1)
                                               else None))
                        g1_t = em.dep("tensor")
                        for i in range(2):
                            t = 2 * u + i
                            em.emit("vector", lambda e, i=i, t=t: e.tensor_copy(
                                simbuf[:, t, :], pSG[:, 16 * i:16 * i + 16]),
                                after=[("tensor", g1_t)], tick=True, selfw=False,
                                set_guard=("pSG_sim" if i == 1 else None))
                        for oc in range(4):
                            em.emit("scalar", lambda e, oc=oc: e.activation(
                                h1scr[:], pG[oc][:], AF.Gelu, bias=b1T[:, oc:oc + 1]),
                                after=[("tensor", g1_t)], guard="h1scr", tick=True,
                                set_guard=f"pG{oc}")
                            em.emit("vector", lambda e, oc=oc: e.tensor_copy(
                                h1r[:, oc, :], h1scr[:]),
                                after=[("scalar", em.dep("scalar"))],
                                guard=("h1r" if oc == 0 else None),
                                tick=True, selfw=False, set_guard="h1scr")
                        h1_v = em.dep("vector")
                        for i in range(2):
                            for kc in range(4):
                                em.emit("tensor", lambda e, i=i, kc=kc: e.matmul(
                                    pSG[:, 32 + 16 * i:48 + 16 * i],
                                    h1r[:, kc, 128 * i:128 * i + 128], W2r[:, kc, :],
                                    start=(kc == 0 and i == 0), stop=False),
                                    after=[("vector", h1_v)],
                                    guard=("pSG_sim" if (kc == 0 and i == 0)
                                           else None))
                            em.emit("tensor", lambda e, i=i: e.matmul(
                                pSG[:, 32 + 16 * i:48 + 16 * i], ones_r[:], b2r[:],
                                start=False, stop=(i == 1)),
                                tick=(i == 1),
                                set_guard=("h1r" if i == 1 else None))
                        g2_t = em.dep("tensor")
                        for i in range(2):
                            t = 2 * u + i
                            em.emit("vector", lambda e, i=i, t=t: e.tensor_copy(
                                h2buf[:, t, :], pSG[:, 32 + 16 * i:48 + 16 * i]),
                                after=[("tensor", g2_t)], tick=True, selfw=False,
                                set_guard=("pSG" if i == 1 else None))
                        yield
                    em.phase = f"b1d_{s}"
                    vq = em.dep("vector")
                    em.emit("scalar", lambda e: e.activation(
                        gbuf[:].rearrange("p a b -> p (a b)"),
                        h2buf[:].rearrange("p a b -> p (a b)"), AF.Sigmoid),
                        after=[("vector", vq)], tick=True)
                    sig_s = em.dep("scalar")
                    iv_bc = statb[:, s, :, 3:4].broadcast_to((128, NTILES, C))
                    mx_bc = emxb[:, :, None].broadcast_to((128, NTILES, C))
                    ri_bc = erinb[:, :, None].broadcast_to((128, NTILES, C))
                    em.emit("vector", lambda e: e.tensor_tensor(
                        lgtb[:], simbuf[:], iv_bc, ALU.mult),
                        after=[("scalar", sig_s)], tick=True)
                    em.emit("vector", lambda e: e.tensor_add(
                        lgtb[:], lgtb[:], gbuf[:]), tick=True)
                    em.emit("vector", lambda e: e.reduce_max(
                        emxb[:, :, None], lgtb[:], axis=AX, negate=True),
                        tick=True)
                    em.emit("vector", lambda e: e.tensor_add(
                        lgtb[:], lgtb[:], mx_bc), tick=True)
                    vq = em.dep("vector")
                    em.emit("scalar", lambda e: e.activation(
                        h2buf[:].rearrange("p a b -> p (a b)"),
                        lgtb[:].rearrange("p a b -> p (a b)"), AF.Exp),
                        after=[("vector", vq)], tick=True)
                    em.emit("vector", lambda e: e.reduce_sum(
                        esumb[:, :, None], h2buf[:], axis=AX),
                        after=[("scalar", em.dep("scalar"))], tick=True)
                    em.emit("vector", lambda e: e.reciprocal(
                        erinb[:], esumb[:]), tick=True)
                    em.emit("vector", lambda e: e.tensor_scalar(
                        gbuf[:], lgtb[:], 0.0, 0.0, ALU.add, ALU.is_ge),
                        tick=True)
                    em.emit("vector", lambda e: e.tensor_mul(
                        gbuf[:], gbuf[:], h2buf[:]), tick=True)
                    em.emit("vector", lambda e: e.tensor_tensor(
                        wbuf[:], gbuf[:], ri_bc, ALU.mult),
                        guard="wbufr", tick=True)
                    yield

            def b2_phase(s, w_v):
                em.phase = f"b2_{s}"
                with ExitStack() as b2_es:
                    selr = b2_es.enter_context(
                        sb_t(f"selr@{s}", [C, C * 128], BF16))
                    pb_bc = b2_es.enter_context(
                        sb_t(f"pb_bc@{s}", [128, D], F32))
                    selg = em.dma("gpsimd", selr[:], sel_d, stream="midg",
                                  after=[("midg", em.dep("midg")),
                                         ("tensor", em.dep("tensor")),
                                         ("vector", em.dep("vector")),
                                         ("scalar", em.dep("scalar"))])
                    pbg = em.dma("sync", pb_bc[:],
                                 pb_d.broadcast_to((128, D)), stream="mids",
                                 after=[("mids", em.dep("mids")),
                                        ("tensor", em.dep("tensor")),
                                        ("vector", em.dep("vector")),
                                        ("scalar", em.dep("scalar"))])
                    xn2tmb = b2_es.enter_context(
                        sb_t(f"xn2tmb@{s}", [128, 2, D], BF16))
                    wcm = b2_es.enter_context(sb_t(f"wcm@{s}", [C, 256], BF16))
                    xn2f = b2_es.enter_context(
                        sb_t(f"xn2f@{s}", [128, KC, 256], BF16))
                    colsw = b2_es.enter_context(
                        sb_t(f"colsw@{s}", [128, 32, 256], BF16))
                    gcol = b2_es.enter_context(
                        sb_t(f"gcol@{s}", [128, 2, 256], F32))
                    pCc = [b2_es.enter_context(
                        ps_t(f"pCc{i}@{s}", [128, 256], F32)) for i in range(3)]
                    pWb = [b2_es.enter_context(
                        ps_t(f"pWb{i}@{s}", [128, 256], F32)) for i in range(2)]
                    pP = [b2_es.enter_context(
                        ps_t(f"pP{i}@{s}", [128, 512], F32)) for i in range(2)]
                    pTc = [pP[i][:].bitcast(BF16)[:, 0:128] for i in range(2)]
                    def b2_prelude(tb, s=s):
                        for i in range(2):
                            t = 2 * tb + i
                            ld = em.dma(
                                "sync", x2l[:, i, :],
                                x2sp_d[s, 128 * t:128 * t + 128, :],
                                stream=f"x2l{i}", guard=f"x2l{i}",
                                after=[("vector", w_v),
                                       ("scalar", em.dep("scalar"))])
                            em.emit("scalar", lambda e, i=i, s=s, t=t: e.activation(
                                xn2tmb[:, i, :], x2l[:, i, :], AF.Identity,
                                bias=statb[:, s, t, 2:3], scale=statb[:, s, t, 1:2]),
                                after=[(f"x2l{i}", ld), ("midg", mid_g),
                                       ("mids", mid_s)],
                                guard=f"xn2tmb{i}", tick=True)
                        apply_s = em.dep("scalar")
                        for kc in range(KC):
                            for i in range(2):
                                em.emit("tensor", lambda e, kc=kc, i=i: e.transpose(
                                    pTc[i], xn2tmb[:, i, 128 * kc:128 * kc + 128],
                                    idb[:]), after=[("scalar", apply_s)],
                                    guard=[f"pP{i}", "xn2f"], tick=True,
                                    set_guard=(f"xn2tmb{i}" if kc == KC - 1 else None))
                                em.emit("vector", lambda e, kc=kc, i=i: e.tensor_copy(
                                    xn2f[:, kc, 128 * i:128 * i + 128], pTc[i]),
                                    after=[("tensor", em.dep("tensor"))],
                                    tick=True, selfw=False, set_guard=f"pP{i}")
                        for i in range(2):
                            t = 2 * tb + i
                            em.emit("tensor", lambda e, t=t: e.transpose(
                                pCc[0][0:16, 0:128], wbuf[:, t, :], idf[:]),
                                after=[("vector", w_v)], guard="pCc0", tick=True,
                                set_guard="wbufr")
                            em.emit("vector", lambda e, i=i: e.tensor_copy(
                                wcm[:, 128 * i:128 * i + 128], pCc[0][0:16, 0:128]),
                                after=[("tensor", em.dep("tensor"))],
                                guard=("wcm" if i == 0 else None),
                                tick=True, set_guard="pCc0")
                        return em.dep("vector")

                    tr_v = b2_prelude(0)
                    for tb in range(NTILES // 2):
                        colsw_v = {}
                        for ncc in range(32):
                            c, half = ncc // 2, ncc % 2
                            for kc in range(KC):
                                em.emit("tensor", lambda e, ncc=ncc, kc=kc: e.matmul(
                                    pCc[ncc % 3][:],
                                    Wcb[:, kc, 128 * ncc:128 * ncc + 128],
                                    xn2f[:, kc, :], start=(kc == 0),
                                    stop=(kc == KC - 1)),
                                    after=[("vector", tr_v), ("midg", selg)],
                                    guard=(f"pCc{ncc % 3}" if kc == 0 else None),
                                    tick=(kc == KC - 1))
                            tq = em.dep("tensor")
                            em.emit("tensor", lambda e, c=c, ncc=ncc: e.matmul(
                                pWb[ncc % 2][:], selr[:, 128 * c:128 * c + 128],
                                wcm[:],
                                start=True, stop=True), guard=f"pWb{ncc % 2}",
                                tick=True)
                            twb = em.dep("tensor")
                            em.emit("scalar",
                                    lambda e, ncc=ncc, c=c, half=half: e.activation(
                                        gcol[:, ncc % 2, :], pCc[ncc % 3][:],
                                        AF.Gelu,
                                        bias=bcT[:, half, c:c + 1]),
                                    after=[("tensor", tq)],
                                    guard=f"gcol{ncc % 2}", tick=True,
                                    selfw=False, set_guard=f"pCc{ncc % 3}")
                            em.emit("vector", lambda e, ncc=ncc: e.tensor_mul(
                                colsw[:, ncc, :], gcol[:, ncc % 2, :],
                                pWb[ncc % 2][:]),
                                after=[("scalar", em.dep("scalar")),
                                       ("tensor", twb)],
                                guard=("colsw" if ncc == 0 else None),
                                tick=True, selfw=False,
                                set_guard=[f"gcol{ncc % 2}", f"pWb{ncc % 2}"])
                            colsw_v[ncc] = em.dep("vector")
                            if ncc >= 2:
                                pj = ncc - 2
                                for i in range(2):
                                    em.emit("tensor",
                                            lambda e, i=i, pj=pj: e.matmul(
                                                pP[i][:],
                                                colsw[:, pj, 128 * i:128 * i + 128],
                                                pwb[:, pj, 0:512],
                                                start=(pj == 0), stop=False),
                                            after=[("vector", colsw_v[pj])],
                                            guard=(f"pP{i}" if pj == 0 else None))
                        for pj in (30, 31):
                            for i in range(2):
                                em.emit("tensor",
                                        lambda e, i=i, pj=pj: e.matmul(
                                            pP[i][:],
                                            colsw[:, pj, 128 * i:128 * i + 128],
                                            pwb[:, pj, 0:512],
                                            start=False, stop=(pj == 31)),
                                        after=[("vector", colsw_v[pj])],
                                        tick=(pj == 31))
                        cols_v = em.dep("vector")
                        em.guards["xn2f"] = ("tensor", em.dep("tensor"))
                        for i in range(2):
                            t = 2 * tb + i
                            em.emit("scalar", lambda e, i=i, s=s, t=t: e.activation(
                                ybuf[:, i, :], x2l[:, i, :], AF.Identity,
                                bias=statb[:, s, t, 2:3], scale=statb[:, s, t, 1:2]),
                                guard=f"ybuf{i}", tick=True,
                                set_guard=f"x2l{i}")
                        yb_s = em.dep("scalar")
                        for oh in range(2):
                            for i in range(2):
                                if oh == 1:
                                    for ncc in range(32):
                                        em.emit("tensor",
                                                lambda e, i=i, ncc=ncc: e.matmul(
                                                    pP[i][:],
                                                    colsw[:, ncc, 128 * i:128 * i + 128],
                                                    pwb[:, ncc, 512:1024],
                                                    start=(ncc == 0), stop=(ncc == 31)),
                                                after=[("vector", cols_v)],
                                                guard=(f"pP{i}" if ncc == 0 else None),
                                                tick=(ncc == 31),
                                                set_guard=("colsw" if (i == 1
                                                                       and ncc == 31)
                                                           else None))
                                em.emit("vector", lambda e, oh=oh, i=i: e.tensor_add(
                                    ybuf[:, i, 512 * oh:512 * oh + 512], pP[i][:],
                                    ybuf[:, i, 512 * oh:512 * oh + 512]),
                                    after=[("tensor", em.dep("tensor")),
                                           ("scalar", yb_s), ("mids", pbg)],
                                    tick=True)
                                em.emit("vector", lambda e, oh=oh, i=i: e.tensor_add(
                                    ybuf[:, i, 512 * oh:512 * oh + 512],
                                    ybuf[:, i, 512 * oh:512 * oh + 512],
                                    pb_bc[:, 512 * oh:512 * oh + 512]),
                                    tick=True, set_guard=f"pP{i}")
                        vq = em.dep("vector")
                        for i in range(2):
                            t = 2 * tb + i
                            em.dma("gpsimd", ysp_d[s, 128 * t:128 * t + 128, :],
                                   ybuf[:, i, :], stream=f"yo{i}",
                                   after=[("vector", vq)],
                                   set_guard=f"ybuf{i}")
                        if tb + 1 < NTILES // 2:
                            tr_v = b2_prelude(tb + 1)

            def gen_b3(s):
                em.phase = f"b3_{s}"
                with ExitStack() as b3_es:
                    yb = b3_es.enter_context(sb_t(f"yb@{s}", [128, 2, D], F32))
                    ynscr = b3_es.enter_context(sb_t(f"ynscr@{s}", [128, 512], F32))
                    bst = b3_es.enter_context(sb_t(f"bst@{s}", [128, 10], F32))
                    brst = b3_es.enter_context(sb_t(f"brst@{s}", [128, 2], F32))
                    bnb2 = b3_es.enter_context(sb_t(f"bnb2@{s}", [128, 2], F32))
                    ysp_tot = [("yo0", em.dep("yo0")), ("yo1", em.dep("yo1")),
                               ("tensor", em.dep("tensor")),
                               ("vector", em.dep("vector")),
                               ("scalar", em.dep("scalar"))]
                    for tp in range(NTILES // 2):
                        em.phase = f"b3_{s}"
                        ylds = []
                        x2lds = []
                        for i in range(2):
                            t = 2 * tp + i
                            ylds.append(em.dma(
                                "sync", yb[:, i, :],
                                ysp_d[s, 128 * t:128 * t + 128, :],
                                stream=f"yi{i}", after=ysp_tot,
                                guard=f"yb{i}"))
                            x2lds.append(em.dma(
                                "sync", x2l[:, i, :],
                                x2sp_d[s, 128 * t:128 * t + 128, :],
                                stream=f"x2m{i}", after=ysp_tot,
                                guard=f"x2l{i}"))
                        for i in range(2):
                            em.emit("vector", lambda e, i=i: e.scalar_tensor_tensor(
                                ynscr[:], yb[:, i, 0:512], 1.0, yb[:, i, 0:512],
                                ALU.mult, ALU.mult, accum_out=bst[:, i:i + 1]),
                                after=[(f"yi{i}", ylds[i])], guard="b3scr",
                                tick=True)
                            em.emit("vector", lambda e, i=i: e.scalar_tensor_tensor(
                                ynscr[:], yb[:, i, 512:1024], 1.0,
                                yb[:, i, 512:1024],
                                ALU.mult, ALU.mult,
                                accum_out=bst[:, 8 + i:9 + i]),
                                tick=True)
                            em.emit("vector", lambda e, i=i: e.reduce_sum(
                                bst[:, 2 + i:3 + i], yb[:, i, :], axis=AX),
                                tick=True, selfw=False)
                        em.emit("vector", lambda e: e.tensor_add(
                            bst[:, 0:2], bst[:, 0:2], bst[:, 8:10]), tick=True)
                        em.emit("vector", lambda e: e.tensor_scalar(
                            bst[:, 4:6], bst[:, 0:2], 1.0 / D, EPS, ALU.mult,
                            ALU.add), tick=True)
                        em.emit("vector", lambda e: e.tensor_scalar_mul(
                            bst[:, 2:4], bst[:, 2:4], -1.0 / D), tick=True)
                        em.emit("vector", lambda e: e.tensor_mul(
                            bst[:, 6:8], bst[:, 2:4], bst[:, 2:4]), tick=True)
                        em.emit("vector", lambda e: e.tensor_sub(
                            bst[:, 4:6], bst[:, 4:6], bst[:, 6:8]), tick=True)
                        vq = em.dep("vector")
                        em.emit("scalar", lambda e: e.activation(
                            bst[:, 6:8], bst[:, 4:6], AF.Ln),
                            after=[("vector", vq),
                                   ("scalar!", em.dep("scalar"))], tick=True)
                        em.emit("scalar", lambda e: e.activation(
                            brst[:], bst[:, 6:8], AF.Exp, scale=-0.5),
                            after=[("scalar!", em.dep("scalar"))], tick=True)
                        em.emit("vector", lambda e: e.tensor_mul(
                            bnb2[:], bst[:, 2:4], brst[:]),
                            after=[("scalar", em.dep("scalar"))], tick=True)
                        vq = em.dep("vector")
                        for i in range(2):
                            t = 2 * tp + i
                            for hf in range(2):
                                em.emit("scalar", lambda e, i=i, hf=hf: e.activation(
                                    ynscr[:], yb[:, i, 512 * hf:512 * hf + 512],
                                    AF.Identity,
                                    bias=bnb2[:, i:i + 1], scale=brst[:, i:i + 1]),
                                    after=[("vector", vq)], guard="ynscr",
                                    tick=True,
                                    set_guard=(["b3scr", f"yb{i}"]
                                               if i == 1 and hf == 1
                                               else (f"yb{i}" if hf == 1
                                                     else None)))
                                sq = em.dep("scalar")
                                em.emit("vector", lambda e, i=i, hf=hf: e.tensor_mul(
                                    ybuf[:, i, 512 * hf:512 * hf + 512], ynscr[:],
                                    g_bc[:, 512 * hf:512 * hf + 512]),
                                    after=[("scalar", sq), ("mids", mid_s)],
                                    guard=(f"ybuf{i}" if hf == 0 else None),
                                    tick=True, set_guard="ynscr")
                                em.emit("vector", lambda e, i=i, hf=hf: e.tensor_add(
                                    ybuf[:, i, 512 * hf:512 * hf + 512],
                                    ybuf[:, i, 512 * hf:512 * hf + 512],
                                    b_bc[:, 512 * hf:512 * hf + 512]),
                                    tick=True)
                            em.emit("vector", lambda e, i=i: e.tensor_add(
                                ybuf[:, i, :], ybuf[:, i, :],
                                x2l[:, i, :]),
                                after=[(f"x2m{i}", x2lds[i])], tick=True,
                                set_guard=f"x2l{i}")
                            vq2 = em.dep("vector")
                            em.dma("gpsimd", out_d[s, 128 * t:128 * t + 128, :],
                                   ybuf[:, i, :], stream=f"out{i}",
                                   after=[("vector", vq2)],
                                   set_guard=f"ybuf{i}")
                        yield

            for _ in gen_b1(0):
                pass
            b2_phase(0, em.dep("vector"))
            g3 = gen_b3(0)
            g1 = gen_b1(1)
            next(g3)
            g3_left = NTILES // 2 - 1
            for _ in range(NTILES // 2 + 1):
                next(g1)
                if g3_left > 0:
                    next(g3)
                    g3_left -= 1
            for _ in g1:
                pass
            for _ in g3:
                pass
            b2_phase(1, em.dep("vector"))
            for _ in gen_b3(1):
                pass

        em.wait_all("sync", ["out0", "out1"])

        with ExitStack() as run_es:
            names = list(em.n) + list(em.streams)
            sems = {nm: run_es.enter_context(nc.semaphore(nm)) for nm in names}
            run_es.enter_context(nc.allow_low_precision(
                reason="float32r operands are the intended precision"))
            block = run_es.enter_context(nc.Block())
            em.replay(block, sems, nc)

    return nc


def kernel(L0_attn_x, L0_attn_Wq, L0_attn_Wk, L0_attn_Wv, L0_attn_Wo,
           L0_ffn_col_emb, L0_ffn_gate_w1, L0_ffn_gate_b1, L0_ffn_gate_w2,
           L0_ffn_gate_b2, L0_ffn_W_cols, L0_ffn_b_cols, L0_ffn_proj_w,
           L0_ffn_proj_b, L0_ffn_ln_g, L0_ffn_ln_b):
    x = np.ascontiguousarray(np.asarray(L0_attn_x, dtype=np.float32))
    B, T, _ = x.shape
    n_cores = 8
    spc = B // n_cores
    nc = build_program(spc, T)
    w = {
        "wq": np.ascontiguousarray(L0_attn_Wq, np.float32),
        "wk": np.ascontiguousarray(L0_attn_Wk, np.float32),
        "wv": np.ascontiguousarray(L0_attn_Wv, np.float32),
        "wo": np.ascontiguousarray(L0_attn_Wo, np.float32),
        "ce": np.ascontiguousarray(L0_ffn_col_emb, np.float32),
        "gw1": np.ascontiguousarray(L0_ffn_gate_w1, np.float32),
        "gb1": np.ascontiguousarray(L0_ffn_gate_b1, np.float32),
        "gw2": np.ascontiguousarray(L0_ffn_gate_w2, np.float32),
        "gb2": np.ascontiguousarray(L0_ffn_gate_b2, np.float32),
        "wc": np.ascontiguousarray(L0_ffn_W_cols, np.float32),
        "bc": np.ascontiguousarray(L0_ffn_b_cols, np.float32),
        "pw": np.ascontiguousarray(L0_ffn_proj_w, np.float32),
        "pb": np.ascontiguousarray(L0_ffn_proj_b, np.float32).reshape(1, -1),
        "lng": np.ascontiguousarray(L0_ffn_ln_g, np.float32).reshape(1, -1),
        "lnb": np.ascontiguousarray(L0_ffn_ln_b, np.float32).reshape(1, -1),
    }
    in_maps = [dict(w, x=np.ascontiguousarray(x[c * spc:(c + 1) * spc]))
               for c in range(n_cores)]
    res = run_bass_kernel_spmd(nc, in_maps, core_ids=list(range(n_cores)))
    return np.concatenate([res.results[c]["out"] for c in range(n_cores)], axis=0)


# version 25 (final)
# speedup vs baseline: 1.0466x; 1.0025x over previous
import numpy as np
import ml_dtypes
import concourse.bass as bass
import concourse.mybir as mybir
from concourse.bass_utils import run_bass_kernel_spmd
from contextlib import ExitStack

F32 = mybir.dt.float32
F32R = mybir.dt.float32r
BF16 = mybir.dt.bfloat16
AF = mybir.ActivationFunctionType
ALU = mybir.AluOpType
AX = mybir.AxisListType.X

D = 1024
KC = 8
H = 16
HD = 64
C = 16
NN = 256
DFF = 4096
DH = 512
EPS = 1e-5


class Em:

    ENG = ("tensor", "vector", "scalar", "sync", "gpsimd")

    def __init__(self):
        self.phase = "init"
        self.items = {e: [] for e in self.ENG}
        self.n = {"tensor": 0, "vector": 0, "scalar": 0}
        self.streams = {}
        self.waited = {e: {} for e in self.ENG}
        self.guards = {}

    def dep(self, sem):
        return self.n[sem] if sem in self.n else self.streams.get(sem, 0)

    def emit(self, eng, fn, after=(), tick=False, guard=None, set_guard=None,
             selfw=True):
        assert eng in ("tensor", "vector", "scalar")
        after = list(after)
        if eng in ("vector", "scalar"):
            if selfw and self.n[eng]:
                after.append((eng + "!", self.n[eng]))
            tick = True
        if guard:
            for g in (guard if isinstance(guard, (list, tuple)) else [guard]):
                if g in self.guards:
                    after.append(self.guards[g])
        waits = []
        for p, v in after:
            force = p.endswith("!")
            p = p.rstrip("!")
            if p == eng and not force:
                continue
            have = self.n[p] if p in self.n else self.streams.get(p, 0)
            assert v <= have, f"forward wait {p}>={v} (have {have})"
            if force or self.waited[eng].get(p, 0) < v:
                waits.append((p, v))
                self.waited[eng][p] = v
        if tick:
            self.n[eng] += 1
        if set_guard:
            assert tick
            for g in (set_guard if isinstance(set_guard, (list, tuple))
                      else [set_guard]):
                self.guards[g] = (eng, self.n[eng])
        self.items[eng].append((fn, waits, tick, eng, self.phase))

    def dma(self, eng, out_ap, in_ap, stream, after=(), guard=None,
            set_guard=None):
        assert eng in ("sync", "gpsimd")
        after = list(after)
        if guard:
            for g in (guard if isinstance(guard, (list, tuple)) else [guard]):
                if g in self.guards:
                    after.append(self.guards[g])
        waits = []
        for p, v in after:
            have = self.n[p] if p in self.n else self.streams.get(p, 0)
            assert v <= have, f"forward wait {p}>={v} (have {have})"
            if self.waited[eng].get(p, 0) < v:
                waits.append((p, v))
                self.waited[eng][p] = v
        self.streams[stream] = self.streams.get(stream, 0) + 16
        if set_guard:
            for g in (set_guard if isinstance(set_guard, (list, tuple))
                      else [set_guard]):
                self.guards[g] = (stream, self.streams[stream])
        self.items[eng].append(
            (lambda e, o=out_ap, i=in_ap: e.dma_start(out=o, in_=i),
             waits, True, stream, self.phase))
        return self.streams[stream]

    def wait_all(self, eng, sems):
        for p in sems:
            v = self.dep(p)
            if v and self.waited[eng].get(p, 0) < v:
                self.items[eng].append(
                    (lambda e, p=p, v=v: None, [(p, v)], False, eng, self.phase))
                self.waited[eng][p] = v

    def replay(self, block, sems, nc=None):
        def runner(name):
            def run(eng_obj):
                cur = None
                for fn, waits, tick, target, phase in self.items[name]:
                    if nc is not None and phase != cur:
                        if cur is not None:
                            nc.leave_named_scope(cur, 0, False)
                        nc.enter_named_scope(phase, False)
                        cur = phase
                    for p, v in waits:
                        eng_obj.wait_ge(sems[p], v)
                    inst = fn(eng_obj)
                    if tick and inst is not None:
                        inst.then_inc(sems[target],
                                      16 if target not in self.n else 1)
                if nc is not None and cur is not None:
                    nc.leave_named_scope(cur, 0, False)
            return run
        block.tensor(runner("tensor"))
        block.vector(runner("vector"))
        block.scalar(runner("scalar"))
        block.sync(runner("sync"))
        block.gpsimd(runner("gpsimd"))


def build_program(SEQ, T, detect_races=False, dbg_spill=False):
    NSUP = T // 512
    NTILES = T // 128
    nc = bass.Bass("TRN2", target_bir_lowering=False,
                   detect_race_conditions=detect_races)

    x_d = nc.dram_tensor("x", [SEQ, T, D], F32, kind="ExternalInput").ap()
    wq_d = nc.dram_tensor("wq", [D, D], F32, kind="ExternalInput").ap()
    wk_d = nc.dram_tensor("wk", [D, D], F32, kind="ExternalInput").ap()
    wv_d = nc.dram_tensor("wv", [D, D], F32, kind="ExternalInput").ap()
    wo_d = nc.dram_tensor("wo", [D, D], F32, kind="ExternalInput").ap()
    ce_d = nc.dram_tensor("ce", [C, D], F32, kind="ExternalInput").ap()
    gw1_d = nc.dram_tensor("gw1", [D, DH], F32, kind="ExternalInput").ap()
    gb1_d = nc.dram_tensor("gb1", [DH], F32, kind="ExternalInput").ap()
    gw2_d = nc.dram_tensor("gw2", [DH, C], F32, kind="ExternalInput").ap()
    gb2_d = nc.dram_tensor("gb2", [C], F32, kind="ExternalInput").ap()
    wc_d = nc.dram_tensor("wc", [C, D, NN], F32, kind="ExternalInput").ap()
    bc_d = nc.dram_tensor("bc", [C, NN], F32, kind="ExternalInput").ap()
    pw_d = nc.dram_tensor("pw", [DFF, D], F32, kind="ExternalInput").ap()
    pb_d = nc.dram_tensor("pb", [1, D], F32, kind="ExternalInput").ap()
    lng_d = nc.dram_tensor("lng", [1, D], F32, kind="ExternalInput").ap()
    lnb_d = nc.dram_tensor("lnb", [1, D], F32, kind="ExternalInput").ap()
    out_d = nc.dram_tensor("out", [SEQ, T, D], F32, kind="ExternalOutput").ap()
    spill_kind = "ExternalOutput" if dbg_spill else "Internal"
    x2sp_d = nc.dram_tensor("x2sp", [SEQ, T, D], F32, kind=spill_kind).ap()
    ysp_d = nc.dram_tensor("ysp", [SEQ, T, D], F32, kind=spill_kind).ap()
    nrm_d = nc.dram_tensor("nrm_d", [16, 512], F32, kind="Internal").ap()
    idb_d = nc.inline_tensor(np.eye(128, dtype=ml_dtypes.bfloat16), "idb_c").ap()
    idf_d = nc.inline_tensor(np.eye(128, dtype=np.float32), "idf_c").ap()
    ones_d = nc.inline_tensor(np.ones((1, 128), dtype=np.float32), "ones_c").ap()
    sel_np = np.zeros((C, C * 128), dtype=np.float32)
    for _c in range(C):
        sel_np[_c, _c * 128:(_c + 1) * 128] = 1.0
    sel_d = nc.inline_tensor(sel_np, "sel_c").ap()

    _names = {}
    _reuse = {}

    def _uniq(name):
        k = _names.get(name, 0)
        _names[name] = k + 1
        return name if k == 0 else f"{name}_{k}"

    class _ReuseGuard:

        def __init__(self, inner, key, space):
            self.inner = inner
            self.key = key
            self.space = space

        def __enter__(self):
            addr = nc.bytes_allocated(self.space)
            h = self.inner.__enter__()
            if self.key in _reuse:
                old_addr, old_h = _reuse[self.key]
                assert old_addr == addr, f"layout drift for {self.key}"
                return old_h
            _reuse[self.key] = (addr, h)
            return h

        def __exit__(self, *a):
            return self.inner.__exit__(*a)

    import concourse.bass_types as _bt

    def sb_t(name, shape, dt=F32):
        return _ReuseGuard(nc.sbuf_tensor(_uniq(name), shape, dt),
                           ("sb", name, tuple(shape), dt), bass.MemorySpace.SBUF)

    def ps_t(name, shape, dt=F32):
        return _ReuseGuard(nc.psum_tensor(_uniq(name), shape, dt),
                           ("ps", name, tuple(shape), dt), bass.MemorySpace.PSUM)

    em = Em()
    es = ExitStack()

    with es:
        def sbuf(name, shape, dt=F32):
            return es.enter_context(sb_t(name, shape, dt))

        idb = sbuf("idb", [128, 128], BF16)
        idf = sbuf("idf", [128, 128], F32)
        ones_r = sbuf("ones_r", [1, 128], F32R)
        b1T = sbuf("b1T", [128, 4], F32)
        b2r = sbuf("b2r", [1, C], F32R)
        bcT = sbuf("bcT", [128, 2, C], F32)
        ceT_r = sbuf("ceT_r", [128, KC, C], F32R)
        statb = sbuf("statb", [128, SEQ, NTILES, 4], F32)
        W1r = sbuf("W1r", [128, KC, DH], F32R)
        W2r = sbuf("W2r", [128, 4, C], F32R)


        with ExitStack() as st_es:
            stage = st_es.enter_context(sb_t("stage", [128, 2, DH], F32))
            ones1 = st_es.enter_context(sb_t("ones1", [1, 128], F32))
            stage2 = st_es.enter_context(sb_t("stage2", [16, 1024], F32))
            sc_st = st_es.enter_context(sb_t("sc_st", [16, 1024], F32))
            ssq16 = st_es.enter_context(sb_t("ssq16", [16, 1], F32))
            l16 = st_es.enter_context(sb_t("l16", [16, 1], F32))
            b1s = st_es.enter_context(sb_t("b1s", [4, 128], F32))
            b2s = st_es.enter_context(sb_t("b2s", [1, C], F32))
            bcs = st_es.enter_context(sb_t("bcs", [16, 256], F32))
            em.dma("sync", idb[:], idb_d, stream="ini")
            em.dma("sync", idf[:], idf_d, stream="ini")
            em.dma("sync", ones1[:], ones_d, stream="ini")
            em.dma("sync", b1s[:], gb1_d.rearrange("(c p) -> c p", p=128),
                   stream="ini")
            em.dma("sync", b2s[:], gb2_d.unsqueeze(0), stream="ini")
            em.dma("sync", bcs[:], bc_d, stream="ini")
            em.dma("sync", stage2[:], ce_d, stream="ini")
            ini = em.dep("ini")
            em.emit("vector", lambda e: e.tensor_copy(ones_r[:], ones1[:]),
                    after=[("ini", ini)], tick=True)
            pTi = st_es.enter_context(ps_t("pTi", [128, 128], F32))

            em.emit("tensor", lambda e: e.transpose(pTi[:, 0:4], b1s[:], idf[0:4, 0:4]),
                    after=[("ini", ini)], tick=True)
            em.emit("vector", lambda e: e.tensor_copy(b1T[:], pTi[:, 0:4]),
                    after=[("tensor", em.dep("tensor"))], tick=True)
            em.emit("vector", lambda e: e.tensor_copy(b2r[:], b2s[:]),
                    after=[("ini", ini)], tick=True)
            for half in range(2):
                em.emit("tensor",
                        lambda e, h=half: e.transpose(
                            pTi[:, 32 + 16 * h:48 + 16 * h],
                            bcs[:, 128 * h:128 * h + 128], idf[0:16, 0:16]),
                        after=[("ini", ini), ("vector", em.dep("vector"))], tick=True)
                em.emit("vector",
                        lambda e, h=half: e.tensor_copy(
                            bcT[:, h, :], pTi[:, 32 + 16 * h:48 + 16 * h]),
                        after=[("tensor", em.dep("tensor"))], tick=True)
            em.emit("vector", lambda e: e.scalar_tensor_tensor(
                sc_st[:], stage2[:], 1.0, stage2[:], ALU.mult, ALU.mult,
                accum_out=ssq16[:]), after=[("ini", ini)], tick=True)
            em.emit("vector", lambda e: e.tensor_scalar_max(ssq16[:], ssq16[:], 1e-30),
                    after=[("vector!", em.dep("vector"))], tick=True)
            em.emit("scalar", lambda e: e.activation(l16[:], ssq16[:], AF.Ln),
                    after=[("vector", em.dep("vector"))], tick=True)
            em.emit("scalar", lambda e: e.activation(l16[:], l16[:], AF.Exp, scale=-0.5),
                    after=[("scalar!", em.dep("scalar"))], tick=True)
            em.emit("vector", lambda e: e.tensor_scalar_mul(stage2[:], stage2[:], l16[:]),
                    after=[("scalar", em.dep("scalar"))], tick=True)
            vq = em.dep("vector")
            for kc in range(KC):
                em.emit("tensor", lambda e, k=kc: e.transpose(
                    pTi[:, 64:80], stage2[:, 128 * k:128 * k + 128], idf[0:16, 0:16]),
                    after=[("vector", vq)], guard="pTi2", tick=True)
                em.emit("vector", lambda e, k=kc: e.tensor_copy(ceT_r[:, k, :],
                                                                pTi[:, 64:80]),
                        after=[("tensor", em.dep("tensor"))], tick=True,
                        set_guard="pTi2")
            for kc in range(KC):
                v = em.dma("sync", stage[:, kc % 2, :],
                           gw1_d[128 * kc:128 * kc + 128, :],
                           stream=f"stg{kc % 2}", guard=f"stage{kc % 2}")
                em.emit("vector", lambda e, k=kc: e.tensor_copy(
                    W1r[:, k, :], stage[:, k % 2, :]),
                    after=[(f"stg{kc % 2}", v)], tick=True,
                    set_guard=f"stage{kc % 2}")
            for kc in range(4):
                v = em.dma("sync", stage[:, kc % 2, 0:C],
                           gw2_d[128 * kc:128 * kc + 128, :],
                           stream=f"stg{kc % 2}", guard=f"stage{kc % 2}")
                em.emit("vector", lambda e, k=kc: e.tensor_copy(
                    W2r[:, k, :], stage[:, k % 2, 0:C]),
                    after=[(f"stg{kc % 2}", v)], tick=True,
                    set_guard=f"stage{kc % 2}")
            init_v = em.dep("vector")
            init_t = em.dep("tensor")
            init_s = em.dep("scalar")

        with ExitStack() as a_es:
            def asb(name, shape, dt=F32):
                return a_es.enter_context(sb_t(name, shape, dt))
            Wq_sb = asb("Wq_sb", [128, KC, D], BF16)
            Wk_sb = asb("Wk_sb", [128, KC, D], BF16)
            Wv_sb = asb("Wv_sb", [128, KC, D], BF16)
            Wo_sb = asb("Wo_sb", [128, KC, D], BF16)
            k_f = asb("k_f", [128, KC, T], BF16)
            v_on = asb("v_on", [128, NTILES, H, 65], BF16)
            x1b = asb("x1b", [128, 4, D], F32)
            q_f = asb("q_f", [128, KC, 512], BF16)
            att_f = asb("att_f", [128, KC, 512], BF16)
            x2b = asb("x2b", [128, 2, D], F32)
            sqscr = asb("sqscr", [128, 512], F32)
            ssqN2 = asb("ssqN2", [128, 16], F32)
            ssqN = asb("ssqN", [128, 16], F32)
            smN = asb("smN", [128, 16], F32)
            m2N = asb("m2N", [128, 16], F32)
            tvN = asb("tvN", [128, 16], F32)
            rvN = asb("rvN", [128, 16], F32)
            vaN = asb("vaN", [128, 16], F32)
            ltN = asb("ltN", [128, 16], F32)

            for w_sb, w_d in ((Wq_sb, wq_d), (Wk_sb, wk_d), (Wv_sb, wv_d),
                              (Wo_sb, wo_d)):
                em.dma("gpsimd", w_sb[:], w_d.rearrange("(kc p) o -> p kc o", p=128),
                       stream="wld", after=[("vector", init_v), ("tensor", init_t)])
            wload = em.dep("wld")

            def ln_big(x_ap, col, after):
                em.emit("vector", lambda e: e.scalar_tensor_tensor(
                    sqscr[:], x_ap[:, 0:512], 1.0, x_ap[:, 0:512], ALU.mult,
                    ALU.mult, accum_out=ssqN[:, col:col + 1]),
                    after=after, tick=True)
                em.emit("vector", lambda e: e.scalar_tensor_tensor(
                    sqscr[:], x_ap[:, 512:1024], 1.0, x_ap[:, 512:1024], ALU.mult,
                    ALU.mult, accum_out=ssqN2[:, col:col + 1]), tick=True)
                em.emit("vector", lambda e: e.reduce_sum(
                    smN[:, col:col + 1], x_ap, axis=AX), tick=True)
                return em.dep("vector")

            def ln_tail(s, t0, nt, want_invn):
                negm = statb[:, s, t0:t0 + nt, 0]
                rstd = statb[:, s, t0:t0 + nt, 1]
                nb = statb[:, s, t0:t0 + nt, 2]
                invn = statb[:, s, t0:t0 + nt, 3]
                em.emit("vector", lambda e: e.tensor_add(
                    ssqN[:, 0:nt], ssqN[:, 0:nt], ssqN2[:, 0:nt]),
                    guard="lnscr", tick=True)
                em.emit("vector", lambda e: e.tensor_scalar(
                    tvN[:, 0:nt], ssqN[:, 0:nt], 1.0 / D, EPS, ALU.mult, ALU.add),
                    tick=True)
                em.emit("vector", lambda e: e.tensor_scalar_mul(
                    negm, smN[:, 0:nt], -1.0 / D), tick=True)
                em.emit("vector", lambda e: e.tensor_mul(m2N[:, 0:nt], negm, negm),
                        tick=True)
                em.emit("vector", lambda e: e.tensor_sub(
                    tvN[:, 0:nt], tvN[:, 0:nt], m2N[:, 0:nt]), tick=True)
                vq = em.dep("vector")
                em.emit("scalar", lambda e: e.activation(ltN[:, 0:nt], tvN[:, 0:nt],
                                                         AF.Ln),
                        after=[("vector", vq), ("scalar!", em.dep("scalar"))],
                        tick=True)
                em.emit("scalar", lambda e: e.activation(rstd, ltN[:, 0:nt], AF.Exp,
                                                         scale=-0.5),
                        after=[("scalar!", em.dep("scalar"))], tick=True,
                        set_guard=(None if want_invn else "lnscr"))
                em.emit("vector", lambda e: e.tensor_mul(nb, negm, rstd),
                        after=[("scalar", em.dep("scalar"))], tick=True)
                if want_invn:
                    em.emit("vector", lambda e: e.reciprocal(rvN[:, 0:nt],
                                                             tvN[:, 0:nt]), tick=True)
                    em.emit("vector", lambda e: e.tensor_scalar_add(
                        vaN[:, 0:nt], tvN[:, 0:nt], -EPS), tick=True)
                    em.emit("vector", lambda e: e.tensor_mul(
                        vaN[:, 0:nt], vaN[:, 0:nt], rvN[:, 0:nt]), tick=True)
                    em.emit("vector", lambda e: e.tensor_scalar(
                        vaN[:, 0:nt], vaN[:, 0:nt], float(D), 1e-24, ALU.mult,
                        ALU.max), tick=True)
                    vq = em.dep("vector")
                    em.emit("scalar", lambda e: e.activation(ltN[:, 0:nt],
                                                             vaN[:, 0:nt], AF.Ln),
                            after=[("vector", vq),
                                   ("scalar!", em.dep("scalar"))], tick=True)
                    em.emit("scalar", lambda e: e.activation(invn, ltN[:, 0:nt],
                                                             AF.Exp, scale=-0.5),
                            after=[("scalar!", em.dep("scalar"))], tick=True,
                            set_guard="lnscr")
                return em.dep("vector"), em.dep("scalar")

            for s in range(SEQ):
                em.phase = f"A{s}"
                em.emit("vector", lambda e: e.memset(v_on[:, :, :, 64:65], 1.0),
                        after=[("tensor", em.dep("tensor"))], tick=True)
                vones = em.dep("vector")

                for u in range(NSUP):
                    with ExitStack() as ps_es:
                        xn1f = ps_es.enter_context(
                            sb_t("xn1f", [128, KC, 512], BF16))
                        xn1tm = ps_es.enter_context(
                            sb_t("xn1tm", [128, 2, D], BF16))
                        vtmp = ps_es.enter_context(sb_t("vtmp", [128, 512], BF16))
                        pA = [ps_es.enter_context(
                            ps_t(f"pA{i}", [128, 512], F32)) for i in range(3)]
                        pTb = [ps_es.enter_context(
                            ps_t(f"pTb{i}", [128, 128], BF16)) for i in range(4)]

                        em.phase = f"qkv{s}_{u}"
                        qev_kc = {}
                        xvs = []
                        for i in range(4):
                            xvs.append(em.dma(
                                "sync", x1b[:, i, :],
                                x_d[s, 512 * u + 128 * i:512 * u + 128 * i + 128, :],
                                stream=f"x1{i}", guard=f"x1b{i}",
                                after=[("scalar", em.dep("scalar"))]))
                        for i in range(4):
                            ln_big(x1b[:, i, :], i, [(f"x1{i}", xvs[i])])
                        vq, sq0 = ln_tail(s, 4 * u, 4, True)
                        for i in range(4):
                            t = 4 * u + i
                            em.emit("scalar", lambda e, i=i, s=s, t=t: e.activation(
                                xn1tm[:, i % 2, :], x1b[:, i, :], AF.Identity,
                                bias=statb[:, s, t, 2:3], scale=statb[:, s, t, 1:2]),
                                after=[("vector", vq), ("scalar", sq0)],
                                guard=f"xn1tm{i % 2}",
                                tick=True)
                            sq = em.dep("scalar")
                            for kc in range(KC):
                                em.emit("tensor", lambda e, i=i, kc=kc: e.transpose(
                                    pTb[kc % 4][:],
                                    xn1tm[:, i % 2, 128 * kc:128 * kc + 128], idb[:]),
                                    after=[("scalar", sq), ("ini", 112)],
                                    guard=f"pTb{kc % 4}", tick=True,
                                    set_guard=(f"xn1tm{i % 2}" if kc == KC - 1 else None))
                                em.emit("vector", lambda e, i=i, kc=kc: e.tensor_copy(
                                    xn1f[:, kc, 128 * i:128 * i + 128], pTb[kc % 4][:]),
                                    after=[("tensor", em.dep("tensor"))], tick=True,
                                    selfw=False, set_guard=f"pTb{kc % 4}")
                                qev_kc[kc] = em.dep("vector")
                        xn1_done = em.dep("vector")

                        for oc in range(KC):
                            for wi, wsb in enumerate((Wq_sb, Wk_sb, Wv_sb)):
                                pb = pA[wi]
                                for kc in range(KC):
                                    em.emit("tensor",
                                            lambda e, wsb=wsb, kc=kc, oc=oc, pb=pb: e.matmul(
                                                pb[:],
                                                wsb[:, kc, 128 * oc:128 * oc + 128],
                                                xn1f[:, kc, :], start=(kc == 0),
                                                stop=(kc == KC - 1)),
                                            after=[("vector", xn1_done),
                                                   ("wld", wload)],
                                            guard=(f"pA{wi}" if kc == 0 else None),
                                            tick=(kc == KC - 1))
                                tq = em.dep("tensor")
                                if wi == 0:
                                    em.emit("vector", lambda e, oc=oc, pb=pb: e.tensor_copy(
                                        q_f[:, oc, :], pb[:]), after=[("tensor", tq)],
                                        tick=True, selfw=False, set_guard=f"pA{wi}")
                                elif wi == 1:
                                    em.emit("vector",
                                            lambda e, oc=oc, u=u, pb=pb: e.tensor_copy(
                                                k_f[:, oc, 512 * u:512 * u + 512], pb[:]),
                                            after=[("tensor", tq)], tick=True,
                                            selfw=False, set_guard=f"pA{wi}")
                                else:
                                    em.emit("vector", lambda e, pb=pb: e.tensor_copy(
                                        vtmp[:], pb[:]), after=[("tensor", tq)],
                                        tick=True, set_guard=f"pA{wi}")
                                    vq = em.dep("vector")
                                    for j in range(4):
                                        em.emit("tensor", lambda e, j=j: e.transpose(
                                            pTb[j][:], vtmp[:, 128 * j:128 * j + 128],
                                            idb[:]),
                                            after=[("vector", vq)],
                                            guard=f"pTb{j}", tick=True)
                                        dstv = v_on[:, 4 * u + j, 2 * oc:2 * oc + 2, 0:64]
                                        em.emit("vector",
                                                lambda e, j=j, dstv=dstv: e.tensor_copy(
                                                    dstv,
                                                    pTb[j][:].rearrange(
                                                        "p (h c) -> p h c", h=2)),
                                                after=[("tensor", em.dep("tensor")),
                                                       ("vector", vones)],
                                                tick=True, selfw=False,
                                                set_guard=f"pTb{j}")
                        qkv_v = em.dep("vector")

                    with ExitStack() as aw_es:
                        em.phase = f"attn{s}_{u}"
                        nrm = aw_es.enter_context(sb_t("nrm", [16, 512], F32))
                        rbw = aw_es.enter_context(sb_t("rbw", [128, 2, 512], F32))
                        scr = aw_es.enter_context(sb_t("scr", [1, 512], F32))
                        tt_ticks = {}

                        def norm_chain():
                            em.emit("vector", lambda e: e.reciprocal(
                                nrm[:], nrm[:]),
                                after=[("nrb0", em.dep("nrb0")),
                                       ("nrb1", em.dep("nrb1"))], tick=True)
                            nd = em.dma("gpsimd", nrm_d[:], nrm[:],
                                        stream="nrb0",
                                        after=[("vector", em.dep("vector"))])
                            for kc in range(KC):
                                gds = []
                                for i2 in range(2):
                                    st = f"nrb{2 + 2 * (kc % 2) + i2}"
                                    gds.append((st, em.dma(
                                        "gpsimd" if i2 == 0 else "sync",
                                        rbw[64 * i2:64 * i2 + 64, kc % 2, :],
                                        nrm_d[2 * kc + i2, None, :].broadcast_to(
                                            (64, 512)),
                                        stream=st,
                                        guard=f"rbw{kc % 2}",
                                        after=[("nrb0", nd)])))
                                em.emit("vector", lambda e, kc=kc: e.tensor_tensor(
                                    att_f[:, kc, :], att_f[:, kc, :],
                                    rbw[:, kc % 2, :],
                                    ALU.mult),
                                    after=gds, tick=True,
                                    set_guard=f"rbw{kc % 2}")
                                tt_ticks[kc] = em.dep("vector")

                        with ExitStack() as ps_es:
                            PT = ps_es.enter_context(
                                sb_t("PT", [128, 3, 1536], BF16))
                            pSc = [ps_es.enter_context(
                                ps_t(f"pSc{i}", [128, 1536], F32))
                                for i in range(2)]
                            pU = [ps_es.enter_context(
                                ps_t(f"pU{i}", [65, 512], F32)) for i in range(2)]
                            items = [(j, 0) for j in range(4 * u)]
                            items += [(4 * u + jl, 128 * jl)
                                      for jl in (0, 1, 3, 2)]
                            slots = []
                            cur, col = [], 0
                            for j, off in items:
                                n = 512 - off
                                c = col
                                if c // 512 != (c + n - 1) // 512:
                                    c = (c // 512 + 1) * 512
                                if c + n > 1536:
                                    slots.append((cur, col))
                                    cur, c = [], 0
                                cur.append((j, off, c))
                                col = c + n
                            slots.append((cur, col))
                            ng = len(slots)

                            def make_tail(h, pu):
                                oc, r0 = h // 2, (h % 2) * 64
                                av_t = em.dep("tensor")

                                def tail():
                                    em.emit("vector", lambda e: e.tensor_copy(
                                        scr[:], pu[64:65, :]),
                                        after=[("tensor", av_t)],
                                        guard="scr0", tick=True)
                                    em.dma("gpsimd", nrm[h:h + 1, :],
                                           scr[:],
                                           stream=f"nrb{h % 2}",
                                           after=[("vector", em.dep("vector"))],
                                           set_guard="scr0")
                                    em.emit("vector", lambda e: e.tensor_copy(
                                        att_f[r0:r0 + 64, oc, :], pu[0:64, :]),
                                        tick=True, set_guard=f"pU{h % 2}")
                                return tail

                            exp_ticks = {}
                            tails = {}
                            flat = [(h, gi) for h in range(H) for gi in range(ng)]
                            nfl = len(flat)
                            LAG = 2
                            for t_ in range(nfl + LAG):
                                if t_ < nfl:
                                    h, gi = flat[t_]
                                    oc, r0 = h // 2, (h % 2) * 64
                                    grp, wid = slots[gi]
                                    for j, off, col in grp:
                                        n = 512 - off
                                        em.emit("tensor",
                                                lambda e, j=j, off=off, n=n, col=col, oc=oc, r0=r0, t_=t_: e.matmul(
                                                    pSc[t_ % 2][:, col:col + n],
                                                    k_f[r0:r0 + 64, oc, 128 * j:128 * j + 128],
                                                    q_f[r0:r0 + 64, oc, off:512],
                                                    start=True, stop=True),
                                                after=[("vector", qkv_v)],
                                                guard=(f"pSc{t_ % 2}" if col == 0
                                                       else None), tick=True)
                                    em.emit("scalar", lambda e, wid=wid, t_=t_: e.activation(
                                        PT[:, t_ % 3, 0:wid], pSc[t_ % 2][:, 0:wid],
                                        AF.Exp, scale=0.125),
                                        after=[("tensor", em.dep("tensor"))],
                                        tick=True, selfw=False,
                                        set_guard=f"pSc{t_ % 2}")
                                    exp_ticks[t_] = em.dep("scalar")
                                if t_ - LAG in tails:
                                    tails.pop(t_ - LAG)[1]()
                                if t_ >= LAG:
                                    ai = t_ - LAG
                                    h, gi = flat[ai]
                                    pu = pU[h % 2]
                                    grp, wid = slots[gi]
                                    for mi, (j, off, col) in enumerate(grp):
                                        n = 512 - off
                                        first = (gi == 0 and mi == 0)
                                        last = (gi == ng - 1 and mi == len(grp) - 1)
                                        em.emit("tensor",
                                                lambda e, j=j, off=off, n=n, col=col, h=h, ai=ai, pu=pu, first=first, last=last: e.matmul(
                                                    pu[:, off:512], v_on[:, j, h, :],
                                                    PT[:, ai % 3, col:col + n],
                                                    start=first, stop=last),
                                                after=[("scalar", exp_ticks[ai])],
                                                guard=(f"pU{h % 2}" if first
                                                       else None),
                                                tick=last)
                                    if gi == ng - 1:
                                        tails[t_] = (h, make_tail(h, pu))
                            for k_ in sorted(tails):
                                tails.pop(k_)[1]()
                            norm_chain()

                        with ExitStack() as ps_es:
                            em.phase = f"wo{s}_{u}"
                            pW = [ps_es.enter_context(
                                ps_t(f"pW{i}", [128, 512], F32)) for i in range(8)]
                            mm_ticks = {}
                            for kc in range(KC):
                                for i in range(4):
                                    for oh in range(2):
                                        pb = pW[2 * i + oh]
                                        em.emit("tensor",
                                                lambda e, i=i, oh=oh, kc=kc, pb=pb: e.matmul(
                                                    pb[:],
                                                    att_f[:, kc, 128 * i:128 * i + 128],
                                                    Wo_sb[:, kc, 512 * oh:512 * oh + 512],
                                                    start=(kc == 0), stop=(kc == KC - 1)),
                                                after=[("vector", tt_ticks[kc])],
                                                guard=(f"pW{2 * i + oh}" if kc == 0
                                                       else None),
                                                tick=(kc == KC - 1))
                                        if kc == KC - 1:
                                            mm_ticks[2 * i + oh] = em.dep("tensor")
                            for i in range(4):
                                t = 4 * u + i
                                for oh in range(2):
                                    pb = pW[2 * i + oh]
                                    em.emit("vector",
                                            lambda e, i=i, oh=oh, pb=pb: e.tensor_add(
                                                x2b[:, i % 2, 512 * oh:512 * oh + 512],
                                                pb[:],
                                                x1b[:, i, 512 * oh:512 * oh + 512]),
                                            after=[("tensor", mm_ticks[2 * i + oh])],
                                            guard=(f"x2b{i % 2}" if oh == 0 else None),
                                            tick=True,
                                            set_guard=([f"pW{2 * i + oh}"]
                                                       + ([f"x1b{i}"] if oh == 1
                                                          else [])))
                                vq = em.dep("vector")
                                em.dma("gpsimd", x2sp_d[s, 128 * t:128 * t + 128, :],
                                       x2b[:, i % 2, :], stream=f"x2o{i % 2}",
                                       after=[("vector", vq)],
                                       set_guard=f"x2b{i % 2}")
                                ln_big(x2b[:, i % 2, :], i, [("vector", vq)])
                            ln_tail(s, 4 * u, 4, True)
            a_t, a_v, a_s = em.dep("tensor"), em.dep("vector"), em.dep("scalar")
            a_sp = [("x2o0", em.dep("x2o0")), ("x2o1", em.dep("x2o1"))]

        with ExitStack() as b_es:
            def bsb(name, shape, dt=F32):
                return b_es.enter_context(sb_t(name, shape, dt))
            Wcb = bsb("Wcb", [128, KC, DFF], BF16)
            pwb = bsb("pwb", [128, 32, D], BF16)
            g_bc = bsb("g_bc", [128, D], F32)
            b_bc = bsb("b_bc", [128, D], F32)
            x2l = bsb("x2l", [128, 2, D], F32)
            wbuf = bsb("wbuf", [128, NTILES, C], F32)
            ybuf = bsb("ybuf", [128, 2, D], F32)
            bsm = bsb("bsm", [128, 1], F32)
            btv = bsb("btv", [128, 1], F32)
            bm2 = bsb("bm2", [128, 1], F32)
            bltv = bsb("bltv", [128, 1], F32)
            brstd = bsb("brstd", [128, 1], F32)
            bnb = bsb("bnb", [128, 1], F32)

            em.phase = "mid"
            mid_after = [("tensor", a_t), ("vector", a_v), ("scalar", a_s),
                         ("x2o0", em.dep("x2o0")), ("x2o1", em.dep("x2o1")),
                         ("x10", em.dep("x10")), ("x11", em.dep("x11")),
                         ("x12", em.dep("x12")), ("x13", em.dep("x13")),
                         ("wld", em.dep("wld")),
                         ("nrb0", em.dep("nrb0")), ("nrb1", em.dep("nrb1")),
                         ("nrb2", em.dep("nrb2")), ("nrb3", em.dep("nrb3")),
                         ("nrb4", em.dep("nrb4")), ("nrb5", em.dep("nrb5"))]
            for kc in range(KC):
                em.dma("gpsimd", Wcb[:, kc, :].rearrange("p (c n) -> p c n", c=C),
                       wc_d.rearrange("c (kc p) n -> p kc c n", p=128)[:, kc, :, :],
                       stream="midg", after=mid_after)
            for g in range(4):
                em.dma("gpsimd", pwb[:, 8 * g:8 * g + 8, :],
                       pw_d.rearrange("(nc p) o -> p nc o", p=128)[:, 8 * g:8 * g + 8, :],
                       stream="midg", after=mid_after)

            em.dma("sync", g_bc[:], lng_d.broadcast_to((128, D)), stream="mids",
                   after=mid_after)
            em.dma("sync", b_bc[:], lnb_d.broadcast_to((128, D)), stream="mids",
                   after=mid_after)
            mid_g = em.dep("midg")
            mid_s = em.dep("mids")

            def gen_b1(s):
                em.phase = f"b1_{s}"
                with ExitStack() as b1_es:
                    h2buf = b1_es.enter_context(sb_t(f"h2buf@{s}", [128, NTILES, C], F32))
                    simbuf = b1_es.enter_context(sb_t(f"simbuf@{s}", [128, NTILES, C], F32))
                    gbuf = b1_es.enter_context(sb_t(f"gbuf@{s}", [128, NTILES, C], F32))
                    lgtb = b1_es.enter_context(
                        sb_t(f"lgtb@{s}", [128, NTILES, C], F32))
                    emxb = b1_es.enter_context(
                        sb_t(f"emxb@{s}", [128, NTILES], F32))
                    esumb = b1_es.enter_context(
                        sb_t(f"esumb@{s}", [128, NTILES], F32))
                    erinb = b1_es.enter_context(
                        sb_t(f"erinb@{s}", [128, NTILES], F32))
                    xn2tm = b1_es.enter_context(
                        sb_t(f"xn2tm@{s}", [128, 2, D], F32))
                    xnr = b1_es.enter_context(
                        sb_t(f"xnr@{s}", [128, KC, 256], F32R))
                    h1scr = b1_es.enter_context(
                        sb_t(f"h1scr@{s}", [128, 256], F32))
                    h1r = b1_es.enter_context(
                        sb_t(f"h1r@{s}", [128, 4, 256], F32R))
                    pG = [b1_es.enter_context(
                        ps_t(f"pG{i}@{s}", [128, 256], F32)) for i in range(4)]
                    pSG = b1_es.enter_context(ps_t(f"pSG@{s}", [128, 64], F32))
                    pTf = [b1_es.enter_context(
                        ps_t(f"pTf{i}@{s}", [128, 128], F32)) for i in range(3)]
                    b1sv, b1ss = em.dep("vector"), em.dep("scalar")
                    for u in range(NTILES // 2):
                        em.phase = f"b1_{s}"
                        for i in range(2):
                            t = 2 * u + i
                            ld = em.dma(
                                "sync", x2l[:, i, :],
                                x2sp_d[s, 128 * t:128 * t + 128, :],
                                stream=f"x2l{i}",
                                after=a_sp + [("vector", b1sv),
                                              ("scalar", b1ss)],
                                guard=f"x2l{i}")
                            em.emit("scalar", lambda e, i=i, s=s, t=t: e.activation(
                                xn2tm[:, i, :], x2l[:, i, :], AF.Identity,
                                bias=statb[:, s, t, 2:3], scale=statb[:, s, t, 1:2]),
                                after=[(f"x2l{i}", ld), ("vector", a_v)],
                                guard=f"xn2tm{i}", tick=True,
                                set_guard=f"x2l{i}")
                        apply_s = em.dep("scalar")
                        ev_kc = {}
                        for kc in range(KC):
                            for i in range(2):
                                em.emit("tensor", lambda e, kc=kc, i=i: e.transpose(
                                    pTf[(2 * kc + i) % 3][:],
                                    xn2tm[:, i, 128 * kc:128 * kc + 128], idf[:]),
                                    after=[("scalar", apply_s)],
                                    guard=f"pTf{(2 * kc + i) % 3}", tick=True,
                                    set_guard=(f"xn2tm{i}" if kc == KC - 1 else None))
                                em.emit("vector", lambda e, kc=kc, i=i: e.tensor_copy(
                                    xnr[:, kc, 128 * i:128 * i + 128],
                                    pTf[(2 * kc + i) % 3][:]),
                                    after=[("tensor", em.dep("tensor"))],
                                    guard=("xnr" if (kc == 0 and i == 0) else None),
                                    tick=True, selfw=False,
                                    set_guard=f"pTf{(2 * kc + i) % 3}")
                            ev_kc[kc] = em.dep("vector")
                        for kc in range(KC):
                            for oc in range(4):
                                em.emit("tensor", lambda e, kc=kc, oc=oc: e.matmul(
                                    pG[oc][:], W1r[:, kc, 128 * oc:128 * oc + 128],
                                    xnr[:, kc, :], start=(kc == 0),
                                    stop=(kc == KC - 1)),
                                    after=[("vector", ev_kc[kc])],
                                    guard=(f"pG{oc}" if kc == 0 else None),
                                    tick=(kc == KC - 1))
                        for kc in range(KC):
                            for i in range(2):
                                em.emit("tensor", lambda e, kc=kc, i=i: e.matmul(
                                    pSG[:, 16 * i:16 * i + 16],
                                    xnr[:, kc, 128 * i:128 * i + 128],
                                    ceT_r[:, kc, :],
                                    start=(kc == 0 and i == 0),
                                    stop=(kc == KC - 1 and i == 1)),
                                    after=[("vector", ev_kc[kc])],
                                    guard=("pSG" if (kc == 0 and i == 0) else None),
                                    tick=(kc == KC - 1 and i == 1),
                                    set_guard=("xnr" if (kc == KC - 1 and i == 1)
                                               else None))
                        g1_t = em.dep("tensor")
                        for i in range(2):
                            t = 2 * u + i
                            em.emit("vector", lambda e, i=i, t=t: e.tensor_copy(
                                simbuf[:, t, :], pSG[:, 16 * i:16 * i + 16]),
                                after=[("tensor", g1_t)], tick=True, selfw=False,
                                set_guard=("pSG_sim" if i == 1 else None))
                        for oc in range(4):
                            em.emit("scalar", lambda e, oc=oc: e.activation(
                                h1scr[:], pG[oc][:], AF.Gelu, bias=b1T[:, oc:oc + 1]),
                                after=[("tensor", g1_t)], guard="h1scr", tick=True,
                                set_guard=f"pG{oc}")
                            em.emit("vector", lambda e, oc=oc: e.tensor_copy(
                                h1r[:, oc, :], h1scr[:]),
                                after=[("scalar", em.dep("scalar"))],
                                guard=("h1r" if oc == 0 else None),
                                tick=True, selfw=False, set_guard="h1scr")
                        h1_v = em.dep("vector")
                        for i in range(2):
                            for kc in range(4):
                                em.emit("tensor", lambda e, i=i, kc=kc: e.matmul(
                                    pSG[:, 32 + 16 * i:48 + 16 * i],
                                    h1r[:, kc, 128 * i:128 * i + 128], W2r[:, kc, :],
                                    start=(kc == 0 and i == 0), stop=False),
                                    after=[("vector", h1_v)],
                                    guard=("pSG_sim" if (kc == 0 and i == 0)
                                           else None))
                            em.emit("tensor", lambda e, i=i: e.matmul(
                                pSG[:, 32 + 16 * i:48 + 16 * i], ones_r[:], b2r[:],
                                start=False, stop=(i == 1)),
                                tick=(i == 1),
                                set_guard=("h1r" if i == 1 else None))
                        g2_t = em.dep("tensor")
                        for i in range(2):
                            t = 2 * u + i
                            em.emit("vector", lambda e, i=i, t=t: e.tensor_copy(
                                h2buf[:, t, :], pSG[:, 32 + 16 * i:48 + 16 * i]),
                                after=[("tensor", g2_t)], tick=True, selfw=False,
                                set_guard=("pSG" if i == 1 else None))
                        yield
                    em.phase = f"b1d_{s}"
                    vq = em.dep("vector")
                    em.emit("scalar", lambda e: e.activation(
                        gbuf[:].rearrange("p a b -> p (a b)"),
                        h2buf[:].rearrange("p a b -> p (a b)"), AF.Sigmoid),
                        after=[("vector", vq)], tick=True)
                    sig_s = em.dep("scalar")
                    iv_bc = statb[:, s, :, 3:4].broadcast_to((128, NTILES, C))
                    mx_bc = emxb[:, :, None].broadcast_to((128, NTILES, C))
                    ri_bc = erinb[:, :, None].broadcast_to((128, NTILES, C))
                    em.emit("vector", lambda e: e.tensor_tensor(
                        lgtb[:], simbuf[:], iv_bc, ALU.mult),
                        after=[("scalar", sig_s)], tick=True)
                    em.emit("vector", lambda e: e.tensor_add(
                        lgtb[:], lgtb[:], gbuf[:]), tick=True)
                    em.emit("vector", lambda e: e.reduce_max(
                        emxb[:, :, None], lgtb[:], axis=AX, negate=True),
                        tick=True)
                    em.emit("vector", lambda e: e.tensor_add(
                        lgtb[:], lgtb[:], mx_bc), tick=True)
                    vq = em.dep("vector")
                    em.emit("scalar", lambda e: e.activation(
                        h2buf[:].rearrange("p a b -> p (a b)"),
                        lgtb[:].rearrange("p a b -> p (a b)"), AF.Exp),
                        after=[("vector", vq)], tick=True)
                    em.emit("vector", lambda e: e.reduce_sum(
                        esumb[:, :, None], h2buf[:], axis=AX),
                        after=[("scalar", em.dep("scalar"))], tick=True)
                    em.emit("vector", lambda e: e.reciprocal(
                        erinb[:], esumb[:]), tick=True)
                    em.emit("vector", lambda e: e.tensor_scalar(
                        gbuf[:], lgtb[:], 0.0, 0.0, ALU.add, ALU.is_ge),
                        tick=True)
                    em.emit("vector", lambda e: e.tensor_mul(
                        gbuf[:], gbuf[:], h2buf[:]), tick=True)
                    em.emit("vector", lambda e: e.tensor_tensor(
                        wbuf[:], gbuf[:], ri_bc, ALU.mult),
                        guard="wbufr", tick=True)
                    yield

            def b2_phase(s, w_v):
                em.phase = f"b2_{s}"
                with ExitStack() as b2_es:
                    selr = b2_es.enter_context(
                        sb_t(f"selr@{s}", [C, C * 128], BF16))
                    pb_bc = b2_es.enter_context(
                        sb_t(f"pb_bc@{s}", [128, D], F32))
                    selg = em.dma("gpsimd", selr[:], sel_d, stream="midg",
                                  after=[("midg", em.dep("midg")),
                                         ("tensor", em.dep("tensor")),
                                         ("vector", em.dep("vector")),
                                         ("scalar", em.dep("scalar"))])
                    pbg = em.dma("sync", pb_bc[:],
                                 pb_d.broadcast_to((128, D)), stream="mids",
                                 after=[("mids", em.dep("mids")),
                                        ("tensor", em.dep("tensor")),
                                        ("vector", em.dep("vector")),
                                        ("scalar", em.dep("scalar"))])
                    xn2tmb = b2_es.enter_context(
                        sb_t(f"xn2tmb@{s}", [128, 2, D], BF16))
                    wcm = b2_es.enter_context(sb_t(f"wcm@{s}", [C, 256], BF16))
                    xn2f = b2_es.enter_context(
                        sb_t(f"xn2f@{s}", [128, KC, 256], BF16))
                    colsw = b2_es.enter_context(
                        sb_t(f"colsw@{s}", [128, 32, 256], BF16))
                    gcol = b2_es.enter_context(
                        sb_t(f"gcol@{s}", [128, 2, 256], F32))
                    pCc = [b2_es.enter_context(
                        ps_t(f"pCc{i}@{s}", [128, 256], F32)) for i in range(3)]
                    pWb = [b2_es.enter_context(
                        ps_t(f"pWb{i}@{s}", [128, 256], F32)) for i in range(2)]
                    pP = [b2_es.enter_context(
                        ps_t(f"pP{i}@{s}", [128, 512], F32)) for i in range(2)]
                    pTc = [pP[i][:].bitcast(BF16)[:, 0:128] for i in range(2)]
                    def b2_prelude(tb, s=s):
                        for i in range(2):
                            t = 2 * tb + i
                            ld = em.dma(
                                "sync", x2l[:, i, :],
                                x2sp_d[s, 128 * t:128 * t + 128, :],
                                stream=f"x2l{i}", guard=f"x2l{i}",
                                after=[("vector", w_v),
                                       ("scalar", em.dep("scalar"))])
                            em.emit("scalar", lambda e, i=i, s=s, t=t: e.activation(
                                xn2tmb[:, i, :], x2l[:, i, :], AF.Identity,
                                bias=statb[:, s, t, 2:3], scale=statb[:, s, t, 1:2]),
                                after=[(f"x2l{i}", ld), ("midg", mid_g),
                                       ("mids", mid_s)],
                                guard=f"xn2tmb{i}", tick=True)
                        apply_s = em.dep("scalar")
                        for kc in range(KC):
                            for i in range(2):
                                em.emit("tensor", lambda e, kc=kc, i=i: e.transpose(
                                    pTc[i], xn2tmb[:, i, 128 * kc:128 * kc + 128],
                                    idb[:]), after=[("scalar", apply_s)],
                                    guard=[f"pP{i}", "xn2f"], tick=True,
                                    set_guard=(f"xn2tmb{i}" if kc == KC - 1 else None))
                                em.emit("vector", lambda e, kc=kc, i=i: e.tensor_copy(
                                    xn2f[:, kc, 128 * i:128 * i + 128], pTc[i]),
                                    after=[("tensor", em.dep("tensor"))],
                                    tick=True, selfw=False, set_guard=f"pP{i}")
                        for i in range(2):
                            t = 2 * tb + i
                            em.emit("tensor", lambda e, t=t: e.transpose(
                                pCc[0][0:16, 0:128], wbuf[:, t, :], idf[:]),
                                after=[("vector", w_v)], guard="pCc0", tick=True,
                                set_guard="wbufr")
                            em.emit("vector", lambda e, i=i: e.tensor_copy(
                                wcm[:, 128 * i:128 * i + 128], pCc[0][0:16, 0:128]),
                                after=[("tensor", em.dep("tensor"))],
                                guard=("wcm" if i == 0 else None),
                                tick=True, set_guard="pCc0")
                        return em.dep("vector")

                    tr_v = b2_prelude(0)
                    for tb in range(NTILES // 2):
                        colsw_v = {}
                        for ncc in range(32):
                            c, half = ncc // 2, ncc % 2
                            for kc in range(KC):
                                em.emit("tensor", lambda e, ncc=ncc, kc=kc: e.matmul(
                                    pCc[ncc % 3][:],
                                    Wcb[:, kc, 128 * ncc:128 * ncc + 128],
                                    xn2f[:, kc, :], start=(kc == 0),
                                    stop=(kc == KC - 1)),
                                    after=[("vector", tr_v), ("midg", selg)],
                                    guard=(f"pCc{ncc % 3}" if kc == 0 else None),
                                    tick=(kc == KC - 1))
                            tq = em.dep("tensor")
                            em.emit("tensor", lambda e, c=c, ncc=ncc: e.matmul(
                                pWb[ncc % 2][:], selr[:, 128 * c:128 * c + 128],
                                wcm[:],
                                start=True, stop=True), guard=f"pWb{ncc % 2}",
                                tick=True)
                            twb = em.dep("tensor")
                            em.emit("scalar",
                                    lambda e, ncc=ncc, c=c, half=half: e.activation(
                                        gcol[:, ncc % 2, :], pCc[ncc % 3][:],
                                        AF.Gelu,
                                        bias=bcT[:, half, c:c + 1]),
                                    after=[("tensor", tq)],
                                    guard=f"gcol{ncc % 2}", tick=True,
                                    selfw=False, set_guard=f"pCc{ncc % 3}")
                            em.emit("vector", lambda e, ncc=ncc: e.tensor_mul(
                                colsw[:, ncc, :], gcol[:, ncc % 2, :],
                                pWb[ncc % 2][:]),
                                after=[("scalar", em.dep("scalar")),
                                       ("tensor", twb)],
                                guard=("colsw" if ncc == 0 else None),
                                tick=True, selfw=False,
                                set_guard=[f"gcol{ncc % 2}", f"pWb{ncc % 2}"])
                            colsw_v[ncc] = em.dep("vector")
                            if ncc >= 2:
                                pj = ncc - 2
                                for i in range(2):
                                    em.emit("tensor",
                                            lambda e, i=i, pj=pj: e.matmul(
                                                pP[i][:],
                                                colsw[:, pj, 128 * i:128 * i + 128],
                                                pwb[:, pj, 0:512],
                                                start=(pj == 0), stop=False),
                                            after=[("vector", colsw_v[pj])],
                                            guard=(f"pP{i}" if pj == 0 else None))
                        for pj in (30, 31):
                            for i in range(2):
                                em.emit("tensor",
                                        lambda e, i=i, pj=pj: e.matmul(
                                            pP[i][:],
                                            colsw[:, pj, 128 * i:128 * i + 128],
                                            pwb[:, pj, 0:512],
                                            start=False, stop=(pj == 31)),
                                        after=[("vector", colsw_v[pj])],
                                        tick=(pj == 31))
                        cols_v = em.dep("vector")
                        em.guards["xn2f"] = ("tensor", em.dep("tensor"))
                        for i in range(2):
                            t = 2 * tb + i
                            em.emit("scalar", lambda e, i=i, s=s, t=t: e.activation(
                                ybuf[:, i, :], x2l[:, i, :], AF.Identity,
                                bias=statb[:, s, t, 2:3], scale=statb[:, s, t, 1:2]),
                                guard=f"ybuf{i}", tick=True,
                                set_guard=f"x2l{i}")
                        yb_s = em.dep("scalar")
                        for oh in range(2):
                            for i in range(2):
                                if oh == 1:
                                    for ncc in range(32):
                                        em.emit("tensor",
                                                lambda e, i=i, ncc=ncc: e.matmul(
                                                    pP[i][:],
                                                    colsw[:, ncc, 128 * i:128 * i + 128],
                                                    pwb[:, ncc, 512:1024],
                                                    start=(ncc == 0), stop=(ncc == 31)),
                                                after=[("vector", cols_v)],
                                                guard=(f"pP{i}" if ncc == 0 else None),
                                                tick=(ncc == 31),
                                                set_guard=("colsw" if (i == 1
                                                                       and ncc == 31)
                                                           else None))
                                em.emit("vector", lambda e, oh=oh, i=i: e.tensor_add(
                                    ybuf[:, i, 512 * oh:512 * oh + 512], pP[i][:],
                                    ybuf[:, i, 512 * oh:512 * oh + 512]),
                                    after=[("tensor", em.dep("tensor")),
                                           ("scalar", yb_s), ("mids", pbg)],
                                    tick=True)
                                em.emit("vector", lambda e, oh=oh, i=i: e.tensor_add(
                                    ybuf[:, i, 512 * oh:512 * oh + 512],
                                    ybuf[:, i, 512 * oh:512 * oh + 512],
                                    pb_bc[:, 512 * oh:512 * oh + 512]),
                                    tick=True, set_guard=f"pP{i}")
                        vq = em.dep("vector")
                        for i in range(2):
                            t = 2 * tb + i
                            em.dma("gpsimd", ysp_d[s, 128 * t:128 * t + 128, :],
                                   ybuf[:, i, :], stream=f"yo{i}",
                                   after=[("vector", vq)],
                                   set_guard=f"ybuf{i}")
                        if tb + 1 < NTILES // 2:
                            tr_v = b2_prelude(tb + 1)

            def gen_b3(s):
                em.phase = f"b3_{s}"
                with ExitStack() as b3_es:
                    yb = b3_es.enter_context(sb_t(f"yb@{s}", [128, 2, D], F32))
                    ynscr = b3_es.enter_context(sb_t(f"ynscr@{s}", [128, 512], F32))
                    bst = b3_es.enter_context(sb_t(f"bst@{s}", [128, 10], F32))
                    brst = b3_es.enter_context(sb_t(f"brst@{s}", [128, 2], F32))
                    bnb2 = b3_es.enter_context(sb_t(f"bnb2@{s}", [128, 2], F32))
                    ysp_tot = [("yo0", em.dep("yo0")), ("yo1", em.dep("yo1")),
                               ("tensor", em.dep("tensor")),
                               ("vector", em.dep("vector")),
                               ("scalar", em.dep("scalar"))]
                    for tp in range(NTILES // 2):
                        em.phase = f"b3_{s}"
                        ylds = []
                        x2lds = []
                        for i in range(2):
                            t = 2 * tp + i
                            ylds.append(em.dma(
                                "sync", yb[:, i, :],
                                ysp_d[s, 128 * t:128 * t + 128, :],
                                stream=f"yi{i}", after=ysp_tot,
                                guard=f"yb{i}"))
                            x2lds.append(em.dma(
                                "sync", x2l[:, i, :],
                                x2sp_d[s, 128 * t:128 * t + 128, :],
                                stream=f"x2m{i}", after=ysp_tot,
                                guard=f"x2l{i}"))
                        for i in range(2):
                            em.emit("vector", lambda e, i=i: e.scalar_tensor_tensor(
                                ynscr[:], yb[:, i, 0:512], 1.0, yb[:, i, 0:512],
                                ALU.mult, ALU.mult, accum_out=bst[:, i:i + 1]),
                                after=[(f"yi{i}", ylds[i])], guard="b3scr",
                                tick=True)
                            em.emit("vector", lambda e, i=i: e.scalar_tensor_tensor(
                                ynscr[:], yb[:, i, 512:1024], 1.0,
                                yb[:, i, 512:1024],
                                ALU.mult, ALU.mult,
                                accum_out=bst[:, 8 + i:9 + i]),
                                tick=True)
                            em.emit("vector", lambda e, i=i: e.reduce_sum(
                                bst[:, 2 + i:3 + i], yb[:, i, :], axis=AX),
                                tick=True, selfw=False)
                        em.emit("vector", lambda e: e.tensor_add(
                            bst[:, 0:2], bst[:, 0:2], bst[:, 8:10]), tick=True)
                        em.emit("vector", lambda e: e.tensor_scalar(
                            bst[:, 4:6], bst[:, 0:2], 1.0 / D, EPS, ALU.mult,
                            ALU.add), tick=True)
                        em.emit("vector", lambda e: e.tensor_scalar_mul(
                            bst[:, 2:4], bst[:, 2:4], -1.0 / D), tick=True)
                        em.emit("vector", lambda e: e.tensor_mul(
                            bst[:, 6:8], bst[:, 2:4], bst[:, 2:4]), tick=True)
                        em.emit("vector", lambda e: e.tensor_sub(
                            bst[:, 4:6], bst[:, 4:6], bst[:, 6:8]), tick=True)
                        vq = em.dep("vector")
                        em.emit("scalar", lambda e: e.activation(
                            bst[:, 6:8], bst[:, 4:6], AF.Ln),
                            after=[("vector", vq),
                                   ("scalar!", em.dep("scalar"))], tick=True)
                        em.emit("scalar", lambda e: e.activation(
                            brst[:], bst[:, 6:8], AF.Exp, scale=-0.5),
                            after=[("scalar!", em.dep("scalar"))], tick=True)
                        em.emit("vector", lambda e: e.tensor_mul(
                            bnb2[:], bst[:, 2:4], brst[:]),
                            after=[("scalar", em.dep("scalar"))], tick=True)
                        vq = em.dep("vector")
                        for i in range(2):
                            t = 2 * tp + i
                            for hf in range(2):
                                em.emit("scalar", lambda e, i=i, hf=hf: e.activation(
                                    ynscr[:], yb[:, i, 512 * hf:512 * hf + 512],
                                    AF.Identity,
                                    bias=bnb2[:, i:i + 1], scale=brst[:, i:i + 1]),
                                    after=[("vector", vq)], guard="ynscr",
                                    tick=True,
                                    set_guard=(["b3scr", f"yb{i}"]
                                               if i == 1 and hf == 1
                                               else (f"yb{i}" if hf == 1
                                                     else None)))
                                sq = em.dep("scalar")
                                em.emit("vector", lambda e, i=i, hf=hf: e.tensor_mul(
                                    ybuf[:, i, 512 * hf:512 * hf + 512], ynscr[:],
                                    g_bc[:, 512 * hf:512 * hf + 512]),
                                    after=[("scalar", sq), ("mids", mid_s)],
                                    guard=(f"ybuf{i}" if hf == 0 else None),
                                    tick=True, set_guard="ynscr")
                                em.emit("vector", lambda e, i=i, hf=hf: e.tensor_add(
                                    ybuf[:, i, 512 * hf:512 * hf + 512],
                                    ybuf[:, i, 512 * hf:512 * hf + 512],
                                    b_bc[:, 512 * hf:512 * hf + 512]),
                                    tick=True)
                            em.emit("vector", lambda e, i=i: e.tensor_add(
                                ybuf[:, i, :], ybuf[:, i, :],
                                x2l[:, i, :]),
                                after=[(f"x2m{i}", x2lds[i])], tick=True,
                                set_guard=f"x2l{i}")
                            vq2 = em.dep("vector")
                            em.dma("gpsimd", out_d[s, 128 * t:128 * t + 128, :],
                                   ybuf[:, i, :], stream=f"out{i}",
                                   after=[("vector", vq2)],
                                   set_guard=f"ybuf{i}")
                        yield

            for _ in gen_b1(0):
                pass
            b2_phase(0, em.dep("vector"))
            g3 = gen_b3(0)
            g1 = gen_b1(1)
            next(g3)
            g3_left = NTILES // 2 - 1
            for _ in range(NTILES // 2 + 1):
                next(g1)
                if g3_left > 0:
                    next(g3)
                    g3_left -= 1
            for _ in g1:
                pass
            for _ in g3:
                pass
            b2_phase(1, em.dep("vector"))
            for _ in gen_b3(1):
                pass

        em.wait_all("sync", ["out0", "out1"])

        with ExitStack() as run_es:
            names = list(em.n) + list(em.streams)
            sems = {nm: run_es.enter_context(nc.semaphore(nm)) for nm in names}
            run_es.enter_context(nc.allow_low_precision(
                reason="float32r operands are the intended precision"))
            block = run_es.enter_context(nc.Block())
            em.replay(block, sems, nc)

    return nc


def kernel(L0_attn_x, L0_attn_Wq, L0_attn_Wk, L0_attn_Wv, L0_attn_Wo,
           L0_ffn_col_emb, L0_ffn_gate_w1, L0_ffn_gate_b1, L0_ffn_gate_w2,
           L0_ffn_gate_b2, L0_ffn_W_cols, L0_ffn_b_cols, L0_ffn_proj_w,
           L0_ffn_proj_b, L0_ffn_ln_g, L0_ffn_ln_b):
    x = np.ascontiguousarray(np.asarray(L0_attn_x, dtype=np.float32))
    B, T, _ = x.shape
    n_cores = 8
    spc = B // n_cores
    nc = build_program(spc, T)
    w = {
        "wq": np.ascontiguousarray(L0_attn_Wq, np.float32),
        "wk": np.ascontiguousarray(L0_attn_Wk, np.float32),
        "wv": np.ascontiguousarray(L0_attn_Wv, np.float32),
        "wo": np.ascontiguousarray(L0_attn_Wo, np.float32),
        "ce": np.ascontiguousarray(L0_ffn_col_emb, np.float32),
        "gw1": np.ascontiguousarray(L0_ffn_gate_w1, np.float32),
        "gb1": np.ascontiguousarray(L0_ffn_gate_b1, np.float32),
        "gw2": np.ascontiguousarray(L0_ffn_gate_w2, np.float32),
        "gb2": np.ascontiguousarray(L0_ffn_gate_b2, np.float32),
        "wc": np.ascontiguousarray(L0_ffn_W_cols, np.float32),
        "bc": np.ascontiguousarray(L0_ffn_b_cols, np.float32),
        "pw": np.ascontiguousarray(L0_ffn_proj_w, np.float32),
        "pb": np.ascontiguousarray(L0_ffn_proj_b, np.float32).reshape(1, -1),
        "lng": np.ascontiguousarray(L0_ffn_ln_g, np.float32).reshape(1, -1),
        "lnb": np.ascontiguousarray(L0_ffn_ln_b, np.float32).reshape(1, -1),
    }
    in_maps = [dict(w, x=np.ascontiguousarray(x[c * spc:(c + 1) * spc]))
               for c in range(n_cores)]
    res = run_bass_kernel_spmd(nc, in_maps, core_ids=list(range(n_cores)))
    return np.concatenate([res.results[c]["out"] for c in range(n_cores)], axis=0)
